# version 24
# baseline (speedup 1.0000x reference)
import contextlib
import numpy as np
import concourse.bass as bass
import concourse.mybir as mybir
from concourse.bass_utils import run_bass_kernel_spmd

F32 = mybir.dt.float32
BF16 = mybir.dt.bfloat16
U8 = mybir.dt.uint8
AF = mybir.ActivationFunctionType
ALU = mybir.AluOpType

D = 2048
NCD = 16
SO = 1024
SF = 2048
H = 16
II = 5632
NIC = 44
NES = 8
EPS_LN = 1e-5
EPS_Z = 1e-6
QLOOK = 7
NQS = QLOOK + 1
RG = [[0, 1], [2, 3], [4, 5], [6, 7]]

ENGINES = ("sync", "gpsimd", "tensor", "vector", "scalar")
DMA_LANES = ("W0", "W1", "W2", "W3", "WB0", "WB1",
             "XW0", "XW1", "XW2", "XW3", "XB0", "XB1", "XB2", "XB3",
             "BK0", "BK1", "CST", "X1S0", "X1S1", "Y0", "Y1", "KVI", "KVO")
SEMS = ("T", "V", "S", "CC") + DMA_LANES
SEM_OWNER = {"T": "tensor", "V": "vector", "S": "scalar"}


class Res:
    __slots__ = ("w", "rs")

    def __init__(self):
        self.w = None
        self.rs = []


class Em:
    def __init__(self):
        self.ops = {e: [] for e in ENGINES}
        self.cnt = {s: 0 for s in SEMS}
        self.lastw = {e: {s: 0 for s in SEMS} for e in ENGINES}
        self.owncnt = {e: {s: 0 for s in SEMS} for e in ENGINES}
        self.lane_res = {lane: Res() for lane in DMA_LANES}

    def emit(self, eng, fn, reads=(), writes=(), waits=(), inc=None):
        wd = {}
        for sem, val in waits:
            wd[sem] = max(wd.get(sem, 0), val)
        for res in reads:
            if res.w:
                sem, val = res.w
                wd[sem] = max(wd.get(sem, 0), val)
        for res in writes:
            if res.w:
                sem, val = res.w
                wd[sem] = max(wd.get(sem, 0), val)
            for sem, val in res.rs:
                wd[sem] = max(wd.get(sem, 0), val)
        ws = []
        for sem, val in wd.items():
            if val <= 0 or val <= self.lastw[eng][sem]:
                continue
            ws.append((sem, val))
            self.lastw[eng][sem] = val
        self.ops[eng].append((fn, ws, inc))
        m = None
        if inc is not None:
            sem, n = inc
            self.cnt[sem] += n
            self.owncnt[eng][sem] += n
            m = (sem, self.cnt[sem])
        if m is not None:
            for res in writes:
                res.w = m
                res.rs = []
            for res in reads:
                res.rs.append(m)
        return m

    def dma(self, eng, out_ap, in_ap, lane, reads=(), writes=(), waits=()):
        return self.emit(eng, lambda e, o=out_ap, i=in_ap: e.dma_start(out=o, in_=i),
                         reads=reads, writes=list(writes) + [self.lane_res[lane]],
                         waits=waits, inc=(lane, 16))


def build_program():
    nc = bass.Bass("TRN2", target_bir_lowering=False)

    xT = nc.dram_tensor("xT", [D, SO], F32, kind="ExternalInput").ap()
    wkvh = nc.dram_tensor("wkvh", [NES * 128, NCD * 512], F32, kind="ExternalInput").ap()
    wqh = nc.dram_tensor("wqh", [8 * 128, NCD * 256], F32, kind="ExternalInput").ap()
    woh = nc.dram_tensor("woh", [8 * 128, NCD * 256], F32, kind="ExternalInput").ap()
    wguh = nc.dram_tensor("wguh", [NIC * 128, NCD * 256], F32, kind="ExternalInput").ap()
    wdnh = nc.dram_tensor("wdnh", [16 * 128, NIC * 128], F32, kind="ExternalInput").ap()
    bkvB_h = nc.dram_tensor("bkvB", [128, NES * 512], F32, kind="ExternalInput").ap()
    bqC_h = nc.dram_tensor("bqC", [128, 16], F32, kind="ExternalInput").ap()
    boC_h = nc.dram_tensor("boC", [128, 16], F32, kind="ExternalInput").ap()
    gsC_h = nc.dram_tensor("gsC", [128, NIC], F32, kind="ExternalInput").ap()
    usC_h = nc.dram_tensor("usC", [128, NIC], F32, kind="ExternalInput").ap()
    dsC_h = nc.dram_tensor("dsC", [128, 16], F32, kind="ExternalInput").ap()
    decC_h = nc.dram_tensor("decC", [128, 8], F32, kind="ExternalInput").ap()
    decB_h = nc.dram_tensor("decB", [128, SO], F32, kind="ExternalInput").ap()
    y = nc.dram_tensor("y", [D, SO], F32, kind="ExternalOutput").ap()
    x1d = nc.dram_tensor("x1d", [D, SO], F32, kind="Internal").ap()
    kvin_t = [nc.dram_tensor("kvin%d" % i, [128, 8 * 129], F32, kind="Internal")
              for i in range(2)]
    kvout_t = [nc.dram_tensor("kvout%d" % i, [128, 8 * 129], F32, kind="Internal")
               for i in range(2)]
    kvin = [t.ap() for t in kvin_t]
    kvout = [t.ap() for t in kvout_t]

    em = Em()
    stack = contextlib.ExitStack()

    AR_BYTES = 107712
    ARENA = nc.alloc_sbuf_tensor("ARENA", [128, AR_BYTES], U8)
    base = nc.lookup_mloc(ARENA).addr

    def at(name, shape, dtype, off):
        return nc.alloc_sbuf_tensor_at(name, shape, dtype, offset=base + off)

    HDN = at("HDN", [128, NIC * 1024], BF16, 0)
    LNX = at("LNX", [128, NCD * 1024], BF16, 0)
    XB16 = at("XB16", [128, NCD * 1024], BF16, 32768)
    KT = at("KT", [128, 8 * 256], BF16, 32768)
    QTB = at("QTB", [128, NQS * 1024], BF16, 36864)
    ZBF = at("ZBF", [128, 2048], F32, 53248)
    XSQ = at("XSQ", [128, 2 * 1024], BF16, 65536)
    KSRF = at("KSRF", [128, H * 128], BF16, 69632)
    KVS = at("KVS", [128, H * 129], F32, 73728)
    KVB = at("KVB", [128, H * 129], BF16, 81984)
    DECB = at("DECB", [128, SO], F32, 86144)
    VTL = at("VTL", [128, 2 * 260], BF16, 90240)
    MSB = at("MSB", [128, SO], BF16, 91328)
    RSB = at("RSB", [128, SO], BF16, 93408)
    VT0B = at("VT0B", [128, SO], BF16, 95488)
    BCS = at("BCS", [128, 2 * 512], F32, 99520)
    VT0 = at("VT0", [128, 1024], F32, 103616)
    assert NQS * 2048 <= 53248 - 36864

    REGB = nc.alloc_sbuf_tensor("REGB", [128, NCD * 1024], BF16)
    WB = nc.alloc_sbuf_tensor("WB", [128, 16384], BF16)
    XW = nc.alloc_sbuf_tensor("XW", [128, 4 * 1024], F32)
    X1W = nc.alloc_sbuf_tensor("X1W", [128, 2 * 1024], F32)
    SILU = nc.alloc_sbuf_tensor("SILU", [128, 1024], F32)
    ONE128 = nc.alloc_sbuf_tensor("ONE128", [128, 128], F32)
    ONE128B = nc.alloc_sbuf_tensor("ONE128B", [128, 128], BF16)
    bqC = nc.alloc_sbuf_tensor("bqC_s", [128, 16], F32)
    boC = nc.alloc_sbuf_tensor("boC_s", [128, 16], F32)
    gsC = nc.alloc_sbuf_tensor("gsC_s", [128, NIC], F32)
    usC = nc.alloc_sbuf_tensor("usC_s", [128, NIC], F32)
    dsC = nc.alloc_sbuf_tensor("dsC_s", [128, 16], F32)
    decC = nc.alloc_sbuf_tensor("decC_s", [128, 8], F32)
    CONSTS = nc.alloc_sbuf_tensor("CONSTS", [128, 4], F32)
    pb0_ = nc.alloc_psum_tensor("pb0", [128, 512], F32)
    pb1_ = nc.alloc_psum_tensor("pb1", [128, 512], F32)
    pb2_ = nc.alloc_psum_tensor("pb2", [128, 512], F32)
    pb3_ = nc.alloc_psum_tensor("pb3", [128, 512], F32)
    pk0_ = nc.alloc_psum_tensor("pk0", [128, 512], F32)
    pk1_ = nc.alloc_psum_tensor("pk1", [128, 512], F32)
    pk2_ = nc.alloc_psum_tensor("pk2", [128, 512], F32)
    pk3_ = nc.alloc_psum_tensor("pk3", [128, 512], F32)

    semmap = {}
    for _sn in SEMS:
        semmap[_sn] = stack.enter_context(nc.semaphore("sem_" + _sn))
    with stack, nc.Block() as block:
        PB = [pb0_, pb1_, pb2_, pb3_]
        PK = [pk0_, pk1_, pk2_, pk3_]
        CEPS5 = CONSTS[:, 0:1]
        CEPS6 = CONSTS[:, 1:2]
        CNEG1 = CONSTS[:, 2:3]
        CNEGH = CONSTS[:, 3:4]

        rPB = [Res() for _ in range(4)]
        rPK = [Res() for _ in range(4)]
        rLNXc = [Res() for _ in range(NCD)]
        rREGBq = [Res() for _ in range(4)]
        rW = [Res() for _ in range(4)]
        rHDN = [Res() for _ in range(NIC)]
        rKT = Res()
        rQTB = [Res() for _ in range(NQS)]
        rLNXq = [Res() for _ in range(4)]
        rZBF = [Res() for _ in range(4)]
        rXB16c = [Res() for _ in range(NCD)]
        rXSQ = [Res(), Res()]
        rKVS = [Res(), Res()]
        rKVB = [Res(), Res()]
        rKSRF = [Res(), Res()]
        rKVIN = [Res(), Res()]
        rDECB = Res()
        rDECC = Res()
        rVTL = [Res(), Res()]
        rMSB = Res()
        rRSB = Res()
        rBCS = [Res(), Res()]
        rVT0a = Res()
        rVT0b = Res()
        rXW = [Res() for _ in range(4)]
        rXBL = [Res() for _ in range(4)]
        rX1W = [Res(), Res()]
        rX1L = [Res(), Res()]
        rX1D = [Res() for _ in range(16)]
        rYL = [Res(), Res()]
        rSILU = Res()
        rCONST = Res()
        rCTAB = Res()

        xw_cur = [0]

        def xw_load(src_ap, reads=(), eng="sync"):
            sl = xw_cur[0] % 4
            xw_cur[0] += 1
            em.dma(eng, XW[:, sl * 1024:(sl + 1) * 1024], src_ap,
                   "XW%d" % sl, reads=reads, writes=[rXW[sl]])
            return XW[:, sl * 1024:(sl + 1) * 1024], rXW[sl]

        def mm(bank_ap, lhsT, rhs, bank_res, reads, start, stop):
            def go(e, b=bank_ap, l=lhsT, r=rhs, st=start, sp=stop):
                return e.matmul(b, l, r, start=st, stop=sp, skip_group_check=True)
            return em.emit("tensor", go, reads=reads, writes=[bank_res], inc=("T", 1))

        def mmgroup(bank_ap, pieces, bank_res, reads):
            n = len(pieces)

            def go(e, b=bank_ap, pcs=pieces):
                last = None
                for i, (l, r) in enumerate(pcs):
                    last = e.matmul(b, l, r, start=(i == 0), stop=(i == n - 1),
                                    skip_group_check=True)
                return last
            return em.emit("tensor", go, reads=reads, writes=[bank_res], inc=("T", 1))

        def mmgroup2(bankA, piecesA, bankB, piecesB, resA, resB, reads):
            nA, nB = len(piecesA), len(piecesB)

            def go(e, a=bankA, b=bankB, pA=piecesA, pB=piecesB):
                for i, (l, r) in enumerate(pA):
                    e.matmul(a, l, r, start=(i == 0), stop=(i == nA - 1),
                             skip_group_check=True)
                last = None
                for i, (l, r) in enumerate(pB):
                    last = e.matmul(b, l, r, start=(i == 0), stop=(i == nB - 1),
                                    skip_group_check=True)
                return last
            return em.emit("tensor", go, reads=reads, writes=[resA, resB],
                           inc=("T", 1))

        def mmmulti(chunks, resl, reads):
            def go(e, cs=chunks):
                last = None
                for b, l, r, st, sp in cs:
                    last = e.matmul(b, l, r, start=st, stop=sp,
                                    skip_group_check=True)
                return last
            return em.emit("tensor", go, reads=reads, writes=resl, inc=("T", 1))

        em.emit("vector", lambda e: e.memset(ONE128[:], 1.0), writes=[rCONST], inc=("V", 1))
        em.emit("vector", lambda e: e.memset(ONE128B[:], 1.0 / 2048.0), writes=[rCONST], inc=("V", 1))
        for slot in range(2):
            for hh in range(2):
                em.emit("vector", lambda e, c=slot * 260 + hh * 130 + 128:
                        e.memset(VTL[:, c:c + 1], 1.0), writes=[rVTL[slot]], inc=("V", 1))
        em.emit("vector", lambda e: e.memset(CEPS5, EPS_LN), writes=[rCONST], inc=("V", 1))
        em.emit("vector", lambda e: e.memset(CEPS6, EPS_Z), writes=[rCONST], inc=("V", 1))
        em.emit("vector", lambda e: e.memset(CNEG1, -1.0), writes=[rCONST], inc=("V", 1))
        em.emit("vector", lambda e: e.memset(CNEGH, -0.5), writes=[rCONST], inc=("V", 1))



        def ln_finalize(banks, bres):
            for th in range(2):
                em.emit("scalar", lambda e, th=th, banks=banks: e.activation(
                    out=VT0[:, th * 512:(th + 1) * 512], in_=banks[th][:, 0:512],
                    func=AF.Square),
                    reads=[bres[th]], writes=[rVT0a], inc=("S", 1))
            for th in range(2):
                em.emit("vector", lambda e, th=th, banks=banks: e.tensor_tensor(
                    out=SILU[:, th * 512:(th + 1) * 512], in0=banks[2 + th][:, 0:512],
                    in1=VT0[:, th * 512:(th + 1) * 512], op=ALU.subtract),
                    reads=[bres[2 + th], rVT0a], writes=[rVT0b, rSILU], inc=("V", 1))
            for th in range(2):
                em.emit("scalar", lambda e, th=th: e.activation(
                    out=SILU[:, th * 512:(th + 1) * 512],
                    in_=SILU[:, th * 512:(th + 1) * 512], func=AF.Ln,
                    bias=CEPS5), reads=[rCONST], writes=[rVT0b, rSILU], inc=("S", 1))
            for th in range(2):
                ts = slice(th * 512, (th + 1) * 512)
                em.emit("scalar", lambda e, ts=ts, th=th: e.activation(
                    out=RSB[:, ts], in_=SILU[:, th * 512:(th + 1) * 512],
                    func=AF.Exp,
                    scale=CNEGH), reads=[rVT0b, rSILU, rCONST], writes=[rRSB], inc=("S", 1))
            for th in range(2):
                ts = slice(th * 512, (th + 1) * 512)
                em.emit("vector", lambda e, th=th, ts=ts, banks=banks: e.tensor_copy(
                    out=MSB[:, ts], in_=banks[th][:, 0:512]),
                    reads=[bres[th]], writes=[rMSB], inc=("V", 1))

        for c in range(NCD):
            sl = c % 2
            em.dma("gpsimd", XB16[:, c * 1024:(c + 1) * 1024],
                   xT[c * 128:(c + 1) * 128, :], "XB%d" % (c % 4),
                   writes=[rXB16c[c], rXBL[c % 4]])
            em.emit("vector", lambda e, c=c, sl=sl: e.tensor_tensor(
                out=XSQ[:, sl * 1024:(sl + 1) * 1024],
                in0=XB16[:, c * 1024:(c + 1) * 1024],
                in1=XB16[:, c * 1024:(c + 1) * 1024], op=ALU.mult),
                reads=[rXB16c[c]], writes=[rXSQ[sl]], inc=("V", 1))
            for th in range(2):
                mm(PB[th][:, 0:512], ONE128B[:],
                   XB16[:, c * 1024 + th * 512: c * 1024 + (th + 1) * 512],
                   rPB[th], reads=[rXB16c[c], rCONST],
                   start=(c == 0), stop=(c == NCD - 1))
            for th in range(2):
                mm(PB[2 + th][:, 0:512], ONE128B[:],
                   XSQ[:, sl * 1024 + th * 512: sl * 1024 + (th + 1) * 512],
                   rPB[2 + th], reads=[rXSQ[sl], rCONST],
                   start=(c == 0), stop=(c == NCD - 1))
        em.dma("gpsimd", WB[:, 0:8192], wkvh[0:128, :], "WB0",
               writes=[rW[0], rW[1]])
        em.dma("gpsimd", WB[:, 8192:16384], wkvh[128:256, :], "WB1",
               writes=[rW[2], rW[3]])
        em.dma("sync", decC[:], decC_h, "CST", writes=[rDECC])
        em.dma("sync", DECB[:], decB_h, "CST", writes=[rDECB])
        em.dma("sync", bqC[:], bqC_h, "CST", writes=[rCTAB])
        em.dma("sync", boC[:], boC_h, "CST", writes=[rCTAB])
        em.dma("sync", gsC[:], gsC_h, "CST", writes=[rCTAB])
        em.dma("sync", usC[:], usC_h, "CST", writes=[rCTAB])
        em.dma("sync", dsC[:], dsC_h, "CST", writes=[rCTAB])
        ln_finalize(PB, rPB)

        for c in range(NCD):
            em.emit("vector", lambda e, c=c: e.tensor_tensor(
                out=VT0B[:], in0=XB16[:, c * 1024:(c + 1) * 1024],
                in1=MSB[:, 0:1024],
                op=ALU.subtract), reads=[rXB16c[c], rMSB],
                writes=[rVT0a, rVT0b], inc=("V", 1))
            em.emit("vector", lambda e, c=c: e.tensor_tensor(
                out=LNX[:, c * 1024:(c + 1) * 1024], in0=VT0B[:],
                in1=RSB[:, 0:1024], op=ALU.mult),
                reads=[rVT0a, rVT0b, rRSB], writes=[rLNXq[c // 4]], inc=("V", 1))

        def cc_trigger(i):
            em.emit("gpsimd", lambda e, i=i: e.collective_compute(
                "AllReduce", ALU.add, replica_groups=RG,
                ins=[kvin_t[i].ap().opt()], outs=[kvout_t[i].ap().opt()]),
                reads=[rKVIN[i]], inc=("CC", 1))

        def bcs_load(s):
            sl2 = s % 2
            em.dma("sync", BCS[:, sl2 * 512:(sl2 + 1) * 512],
                   bkvB_h[:, s * 512:(s + 1) * 512], "BK%d" % sl2,
                   writes=[rBCS[sl2]])

        bcs_load(0)
        for s in range(NES):
            sl2 = s % 2
            if s + 1 < NES:
                bcs_load(s + 1)
            bigs = s % 2
            wslot = WB[:, bigs * 8192:(bigs + 1) * 8192]
            wrs = [rW[2 * bigs], rW[2 * bigs + 1]]
            if s == 6:
                cc_trigger(0)
            if s >= 2:
                em.dma("gpsimd", wslot, wkvh[s * 128:(s + 1) * 128, :],
                       "WB%d" % bigs, writes=wrs)

            def wk(c, w=wslot):
                return w[:, c * 512: c * 512 + 256]

            def wv(c, w=wslot):
                return w[:, c * 512 + 256: (c + 1) * 512]

            def kpieces(tch, c0, c1, wslot=wslot):
                kbank = PB[tch % 2]
                return [(kbank[:, 0:256],
                         LNX[:, c * 1024 + tch * 128: c * 1024 + tch * 128 + 128],
                         wk(c, wslot), c == 0, c == NCD - 1)
                        for c in range(c0, c1)]

            def vpieces(tch, c0, c1, wslot=wslot):
                vbank = PB[2 + tch % 2]
                return [(vbank[:, 0:256],
                         LNX[:, c * 1024 + tch * 128: c * 1024 + tch * 128 + 128],
                         wv(c, wslot), c == 0, c == NCD - 1)
                        for c in range(c0, c1)]

            def k_evac(tch, sl2=sl2):
                kbank = PB[tch % 2]
                em.emit("vector", lambda e, kbank=kbank, sl2=sl2: e.tensor_tensor(
                    out=VT0[:, 0:256], in0=kbank[:, 0:256],
                    in1=BCS[:, sl2 * 512: sl2 * 512 + 256],
                    op=ALU.add), reads=[rPB[tch % 2], rBCS[sl2]],
                    writes=[rVT0a], inc=("V", 1))
                em.emit("scalar", lambda e, tch=tch: e.activation(
                    out=KT[:, tch * 256:(tch + 1) * 256], in_=VT0[:, 0:256],
                    func=AF.Relu, scale=decC[:, tch:tch + 1]),
                    reads=[rVT0a, rDECC], writes=[rKT], inc=("S", 1))

            def v_evac(tch, sl2=sl2):
                vslot = tch % 2
                vbank = PB[2 + tch % 2]
                em.emit("vector", lambda e, vbank=vbank, vbase=vslot * 260, sl2=sl2: e.tensor_tensor(
                    out=VTL[:, vbase:vbase + 260].rearrange(
                        "p (h e) -> p h e", h=2)[:, :, 0:128],
                    in0=vbank[:, 0:256].rearrange("p (h e) -> p h e", h=2),
                    in1=BCS[:, sl2 * 512 + 256: sl2 * 512 + 512].rearrange(
                        "p (h e) -> p h e", h=2),
                    op=ALU.add), reads=[rPB[2 + tch % 2], rBCS[sl2]],
                    writes=[rVTL[vslot]], inc=("V", 1))

            def kv_mms(tch, s=s):
                vbase = (tch % 2) * 260
                for hh in range(2):
                    hg = s * 2 + hh
                    mm(PK[hg % 4][:, 0:129],
                       KT[:, tch * 256 + hh * 128: tch * 256 + hh * 128 + 128],
                       VTL[:, vbase + hh * 130: vbase + hh * 130 + 129],
                       rPK[hg % 4], reads=[rKT, rVTL[tch % 2]],
                       start=(tch == 0), stop=(tch == 7))

            if s == 0:
                for q in range(4):
                    for tch in (0, 1):
                        mmmulti(kpieces(tch, 4 * q, 4 * q + 4),
                                [rPB[tch % 2]], reads=[rLNXq[q]] + wrs)
                    for tch in (0, 1):
                        mmmulti(vpieces(tch, 4 * q, 4 * q + 4),
                                [rPB[2 + tch % 2]], reads=[rLNXq[q]] + wrs)
                for tch in (0, 1):
                    k_evac(tch)
                for tch in (0, 1):
                    v_evac(tch)
                for tch in range(2, 8):
                    mmmulti(kpieces(tch, 0, NCD), [rPB[tch % 2]],
                            reads=rLNXq + wrs)
                    k_evac(tch)
                kv_mms(0)
                for tch in range(2, 8):
                    mmmulti(vpieces(tch, 0, NCD), [rPB[2 + tch % 2]],
                            reads=rLNXq + wrs)
                    v_evac(tch)
                    kv_mms(tch - 1)
                kv_mms(7)
            else:
                for tch in range(8):
                    mmmulti(kpieces(tch, 0, NCD), [rPB[tch % 2]],
                            reads=rLNXq + wrs)
                    k_evac(tch)
                for tch in range(8):
                    mmmulti(vpieces(tch, 0, NCD), [rPB[2 + tch % 2]],
                            reads=rLNXq + wrs)
                    v_evac(tch)
                    if tch > 0:
                        kv_mms(tch - 1)
                kv_mms(7)
            for hh in range(2):
                hg = s * 2 + hh
                cs = slice(hg * 129, hg * 129 + 129)
                em.emit("vector", lambda e, hg=hg, cs=cs: e.tensor_copy(
                    out=KVS[:, cs], in_=PK[hg % 4][:, 0:129]),
                    reads=[rPK[hg % 4]], writes=[rKVS[s // 4]], inc=("V", 1))
            if s == 3:
                em.dma("sync", kvin[0], KVS[:, 0:1032], "KVI",
                       reads=[rKVS[0]], writes=[rKVIN[0]])

        def q_dma(g):
            sl = g % 4
            em.dma("gpsimd", WB[:, sl * 4096:(sl + 1) * 4096],
                   wqh[g * 128:(g + 1) * 128, :], "W%d" % sl, writes=[rW[sl]])

        q_dma(0)
        q_dma(1)
        q_dma(2)

        em.dma("sync", kvin[1], KVS[:, 1032:2064], "KVI",
               reads=[rKVS[1]], writes=[rKVIN[1]])
        cc_trigger(1)
        em.dma("scalar", KVS[:, 0:1032], kvout[0], "KVO", waits=[("CC", 1)],
               writes=[rKVS[0]])
        em.dma("sync", KVS[:, 1032:2064], kvout[1], "KVO", waits=[("CC", 2)],
               writes=[rKVS[1]])

        def q_stage(h):
            if h % 2 == 0 and h >= 6:
                q_dma(h // 2)
            sl = (h // 2) % 4
            qs = h % NQS
            qb = 2 * (h % 2)
            pcs = [[(WB[:, sl * 4096 + c * 256 + (h % 2) * 128:
                        sl * 4096 + c * 256 + (h % 2) * 128 + 128],
                     LNX[:, c * 1024 + th * 512: c * 1024 + th * 512 + 512])
                    for c in range(NCD)] for th in range(2)]
            mmgroup2(PB[qb][:, 0:512], pcs[0], PB[qb + 1][:, 0:512], pcs[1],
                     rPB[qb], rPB[qb + 1], reads=[rW[sl]] + rLNXq)
            for th in range(2):
                vt = rVT0a if th == 0 else rVT0b
                em.emit("scalar", lambda e, th=th, h=h, qb=qb: e.activation(
                    out=VT0[:, th * 512:(th + 1) * 512], in_=PB[qb + th][:, 0:512],
                    func=AF.Relu, bias=bqC[:, h:h + 1]),
                    reads=[rPB[qb + th], rCTAB], writes=[vt], inc=("S", 1))
                em.emit("vector", lambda e, th=th, qs=qs: e.tensor_tensor(
                    out=QTB[:, qs * 1024 + th * 512: qs * 1024 + th * 512 + 512],
                    in0=VT0[:, th * 512:(th + 1) * 512],
                    in1=DECB[:, th * 512:(th + 1) * 512], op=ALU.mult),
                    reads=[vt, rDECB], writes=[rQTB[qs]], inc=("V", 1))

        def attn_stage(h):
            qs = h % NQS
            hf = h // 8
            for th in range(2):
                qsl = slice(qs * 1024 + th * 512, qs * 1024 + th * 512 + 512)
                mm(PK[2 + th][:, 0:512], KSRF[:, h * 128:(h + 1) * 128],
                   QTB[:, qsl], rPK[2 + th],
                   reads=[rKSRF[hf], rQTB[qs]], start=True, stop=True)
            for th in range(2):
                zs = th + 2 * (h % 2)
                em.emit("scalar", lambda e, th=th, zs=zs: e.activation(
                    out=ZBF[:, zs * 512:(zs + 1) * 512], in_=PK[2 + th][:, 0:512],
                    func=AF.Ln, bias=CEPS6),
                    reads=[rPK[2 + th], rCONST], writes=[rZBF[zs]], inc=("S", 1))
            for th in range(2):
                zs = th + 2 * (h % 2)
                em.emit("scalar", lambda e, zs=zs: e.activation(
                    out=ZBF[:, zs * 512:(zs + 1) * 512],
                    in_=ZBF[:, zs * 512:(zs + 1) * 512],
                    func=AF.Exp, scale=CNEG1),
                    reads=[rCONST], writes=[rZBF[zs]], inc=("S", 1))
            for th in range(2):
                qsl = slice(qs * 1024 + th * 512, qs * 1024 + th * 512 + 512)
                mm(PK[th][:, 0:512], KVB[:, h * 129: h * 129 + 128], QTB[:, qsl],
                   rPK[th], reads=[rKVB[hf], rQTB[qs]], start=True, stop=True)
            for th in range(2):
                zs = th + 2 * (h % 2)
                em.emit("vector", lambda e, th=th, h=h, zs=zs: e.tensor_tensor(
                    out=REGB[:, h * 1024 + th * 512: h * 1024 + th * 512 + 512],
                    in0=PK[th][:, 0:512], in1=ZBF[:, zs * 512:(zs + 1) * 512],
                    op=ALU.mult), reads=[rPK[th], rZBF[zs]],
                    writes=[rREGBq[h // 4]], inc=("V", 1))

        def kv_finish(hf):
            em.emit("scalar", lambda e, hf=hf: e.activation(
                out=KVB[:, hf * 1032:(hf + 1) * 1032],
                in_=KVS[:, hf * 1032:(hf + 1) * 1032], func=AF.Identity),
                reads=[rKVS[hf]], writes=[rKVB[hf]], inc=("S", 1))
            for h in range(hf * 8, hf * 8 + 8):
                em.emit("vector", lambda e, h=h: e.tensor_scalar(
                    out=KSRF[:, h * 128:(h + 1) * 128], in0=ONE128[:],
                    scalar1=KVS[:, h * 129 + 128: h * 129 + 129],
                    scalar2=None, op0=ALU.mult),
                    reads=[rKVS[hf], rCONST], writes=[rKSRF[hf]], inc=("V", 1))

        kv_finish(0)
        for h in range(QLOOK + 1):
            q_stage(h)
        for h in range(H):
            attn_stage(h)
            if h == 3:
                kv_finish(1)
            if h + QLOOK + 1 < H:
                q_stage(h + QLOOK + 1)

        def x1_stats(j):
            x1sl = j % 2
            for th in range(2):
                mm(PK[th][:, 0:512], ONE128B[:],
                   LNX[:, j * 1024 + th * 512: j * 1024 + (th + 1) * 512],
                   rPK[th], reads=[rLNXc[j], rCONST],
                   start=(j == 0), stop=(j == NCD - 1))
            for th in range(2):
                mm(PK[2 + th][:, 0:512], ONE128B[:],
                   XSQ[:, x1sl * 1024 + th * 512: x1sl * 1024 + (th + 1) * 512],
                   rPK[2 + th], reads=[rXSQ[x1sl], rCONST],
                   start=(j == 0), stop=(j == NCD - 1))

        for j in range(NCD):
            if j % 2 == 0:
                g2 = j // 2
                sl = g2 % 2
                em.dma("gpsimd", WB[:, sl * 4096:(sl + 1) * 4096],
                       woh[g2 * 128:(g2 + 1) * 128, :], "W%d" % sl,
                       writes=[rW[sl]])
            sl = (j // 2) % 2
            ob = 2 * (j % 2)
            xwap, rxw = xw_load(xT[j * 128:(j + 1) * 128, :], eng="scalar")
            pcs = [[(WB[:, sl * 4096 + c * 256 + (j % 2) * 128:
                        sl * 4096 + c * 256 + (j % 2) * 128 + 128],
                     REGB[:, c * 1024 + th * 512: c * 1024 + th * 512 + 512])
                    for c in range(NCD)] for th in range(2)]
            mmgroup2(PB[ob][:, 0:512], pcs[0], PB[ob + 1][:, 0:512], pcs[1],
                     rPB[ob], rPB[ob + 1], reads=[rW[sl]] + rREGBq)
            if j > 0:
                x1_stats(j - 1)
            x1sl = j % 2
            for th in range(2):
                vt = rVT0a if th == 0 else rVT0b
                em.emit("scalar", lambda e, th=th, j=j, ob=ob: e.activation(
                    out=VT0[:, th * 512:(th + 1) * 512], in_=PB[ob + th][:, 0:512],
                    func=AF.Identity, bias=boC[:, j:j + 1]),
                    reads=[rPB[ob + th], rCTAB], writes=[vt], inc=("S", 1))
                em.emit("vector", lambda e, th=th, x1sl=x1sl, xwap=xwap: e.tensor_tensor(
                    out=X1W[:, x1sl * 1024 + th * 512: x1sl * 1024 + (th + 1) * 512],
                    in0=VT0[:, th * 512:(th + 1) * 512],
                    in1=xwap[:, th * 512:(th + 1) * 512], op=ALU.add),
                    reads=[vt, rxw], writes=[rX1W[x1sl]], inc=("V", 1))
            em.dma("sync", x1d[j * 128:(j + 1) * 128, :],
                   X1W[:, x1sl * 1024:(x1sl + 1) * 1024], "X1S%d" % x1sl,
                   reads=[rX1W[x1sl]], writes=[rX1L[x1sl], rX1D[j]])
            em.emit("vector", lambda e, j=j, x1sl=x1sl: e.tensor_copy(
                out=LNX[:, j * 1024:(j + 1) * 1024],
                in_=X1W[:, x1sl * 1024:(x1sl + 1) * 1024]),
                reads=[rX1W[x1sl]], writes=[rLNXc[j]], inc=("V", 1))
            em.emit("vector", lambda e, j=j, x1sl=x1sl: e.tensor_tensor(
                out=XSQ[:, x1sl * 1024:(x1sl + 1) * 1024],
                in0=LNX[:, j * 1024:(j + 1) * 1024],
                in1=LNX[:, j * 1024:(j + 1) * 1024], op=ALU.mult),
                reads=[rLNXc[j]], writes=[rXSQ[x1sl]], inc=("V", 1))
        x1_stats(NCD - 1)
        ln_finalize(PK, rPK)

        for c in range(NCD):
            em.emit("vector", lambda e, c=c: e.tensor_tensor(
                out=VT0B[:], in0=LNX[:, c * 1024:(c + 1) * 1024],
                in1=MSB[:, 0:1024],
                op=ALU.subtract), reads=[rLNXc[c], rMSB],
                writes=[rVT0a, rVT0b], inc=("V", 1))
            em.emit("vector", lambda e, c=c: e.tensor_tensor(
                out=REGB[:, c * 1024:(c + 1) * 1024], in0=VT0B[:],
                in1=RSB[:, 0:1024], op=ALU.mult),
                reads=[rVT0a, rVT0b, rRSB], writes=[rREGBq[c // 4]], inc=("V", 1))

        def wgu_dma(ic):
            sl = 2 + ic % 2
            em.dma("gpsimd", WB[:, sl * 4096:(sl + 1) * 4096],
                   wguh[ic * 128:(ic + 1) * 128, :], "W%d" % sl,
                   writes=[rW[sl]])

        wgu_dma(0)
        wgu_dma(1)
        em.dma("gpsimd", WB[:, 0:5632], wdnh[0:128, :], "WB0",
               writes=[rW[0], rW[1]])
        for ic in range(NIC):
            sl = 2 + ic % 2
            if ic >= 2:
                wgu_dma(ic)
            a = 2 * (ic % 2)
            G = [PB[a], PB[a + 1]]
            U = [PK[a], PK[a + 1]]

            def gu_chunks(c0, c1):
                chunks = []
                for c in range(c0, c1):
                    gap = WB[:, sl * 4096 + c * 256: sl * 4096 + c * 256 + 128]
                    uap = WB[:, sl * 4096 + c * 256 + 128: sl * 4096 + (c + 1) * 256]
                    st, sp = (c == 0), (c == NCD - 1)
                    for th in range(2):
                        chunks.append((G[th][:, 0:512], gap,
                                       REGB[:, c * 1024 + th * 512: c * 1024 + (th + 1) * 512],
                                       st, sp))
                    for th in range(2):
                        chunks.append((U[th][:, 0:512], uap,
                                       REGB[:, c * 1024 + th * 512: c * 1024 + (th + 1) * 512],
                                       st, sp))
                return chunks
            bres = [rPB[a], rPB[a + 1], rPK[a], rPK[a + 1]]
            if ic <= 1:
                for q in range(4):
                    mmmulti(gu_chunks(4 * q, 4 * q + 4), bres,
                            reads=[rW[sl], rREGBq[q]])
            else:
                mmmulti(gu_chunks(0, NCD), bres, reads=[rW[sl]] + rREGBq)
            for th in range(2):
                em.emit("scalar", lambda e, th=th, ic=ic, G=G: e.activation(
                    out=SILU[:, th * 512:(th + 1) * 512], in_=G[th][:, 0:512],
                    func=AF.Silu, scale=gsC[:, ic:ic + 1]),
                    reads=[rPB[a + th], rCTAB], writes=[rSILU], inc=("S", 1))
            for th in range(2):
                em.emit("vector", lambda e, th=th, ic=ic, U=U: e.scalar_tensor_tensor(
                    out=HDN[:, ic * 1024 + th * 512: ic * 1024 + (th + 1) * 512],
                    in0=U[th][:, 0:512], scalar=usC[:, ic:ic + 1],
                    in1=SILU[:, th * 512:(th + 1) * 512],
                    op0=ALU.mult, op1=ALU.mult),
                    reads=[rPK[a + th], rSILU, rCTAB], writes=[rHDN[ic]],
                    inc=("V", 1))

        em.dma("gpsimd", WB[:, 8192: 8192 + 5632], wdnh[128:256, :], "WB1",
               writes=[rW[2], rW[3]])
        BPAIRS = [(PB[0], PB[1], rPB[0], rPB[1]), (PB[2], PB[3], rPB[2], rPB[3]),
                  (PK[0], PK[1], rPK[0], rPK[1]), (PK[2], PK[3], rPK[2], rPK[3])]
        for j in range(NCD):
            bigs = j % 2
            if j >= 2:
                em.dma("gpsimd", WB[:, bigs * 8192: bigs * 8192 + 5632],
                       wdnh[j * 128:(j + 1) * 128, :], "WB%d" % bigs,
                       writes=[rW[2 * bigs], rW[2 * bigs + 1]])
            wrs = [rW[2 * bigs], rW[2 * bigs + 1]]
            xwap, rxw = xw_load(x1d[j * 128:(j + 1) * 128, :], reads=[rX1D[j]])
            b0, b1, rb0, rb1 = BPAIRS[j % 4]
            chunks = []
            for ic in range(NIC):
                dap = WB[:, bigs * 8192 + ic * 128: bigs * 8192 + (ic + 1) * 128]
                st, sp = (ic == 0), (ic == NIC - 1)
                chunks.append((b0[:, 0:512], dap, HDN[:, ic * 1024: ic * 1024 + 512],
                               st, sp))
                chunks.append((b1[:, 0:512], dap, HDN[:, ic * 1024 + 512: (ic + 1) * 1024],
                               st, sp))
            mmmulti(chunks, [rb0, rb1], reads=wrs + rHDN)
            x1sl = j % 2
            for th in range(2):
                bb = b0 if th == 0 else b1
                rb = rb0 if th == 0 else rb1
                em.emit("vector", lambda e, th=th, j=j, x1sl=x1sl, bb=bb, xwap=xwap: e.scalar_tensor_tensor(
                    out=X1W[:, x1sl * 1024 + th * 512: x1sl * 1024 + (th + 1) * 512],
                    in0=bb[:, 0:512], scalar=dsC[:, j:j + 1],
                    in1=xwap[:, th * 512:(th + 1) * 512],
                    op0=ALU.mult, op1=ALU.add),
                    reads=[rb, rxw, rCTAB], writes=[rX1W[x1sl]], inc=("V", 1))
            em.dma("sync", y[j * 128:(j + 1) * 128, :],
                   X1W[:, x1sl * 1024:(x1sl + 1) * 1024], "Y%d" % (j % 2),
                   reads=[rX1W[x1sl]], writes=[rYL[j % 2]])

        fin = [(lane, em.cnt[lane]) for lane in ("Y0", "Y1") if em.cnt[lane] > 0]

        def emit_engine(eng_name):
            def go(e):
                for fn, ws, inc in em.ops[eng_name]:
                    for sem, val in ws:
                        e.wait_ge(semmap[sem], val)
                    inst = fn(e)
                    if inc is not None:
                        inst.then_inc(semmap[inc[0]], inc[1])
                if eng_name == "sync":
                    for lane, val in fin:
                        e.wait_ge(semmap[lane], val)
            return go

        block.sync(emit_engine("sync"))
        block.gpsimd(emit_engine("gpsimd"))
        block.tensor(emit_engine("tensor"))
        block.vector(emit_engine("vector"))
        block.scalar(emit_engine("scalar"))

    return nc


_NC_CACHE = {}


def kernel(**inputs):
    x = np.asarray(inputs["L0_attn_x"], dtype=np.float32)
    Wq = np.asarray(inputs["L0_attn_Wq"], dtype=np.float32)
    bq = np.asarray(inputs["L0_attn_bq"], dtype=np.float32)
    Wk = np.asarray(inputs["L0_attn_Wk"], dtype=np.float32)
    bk = np.asarray(inputs["L0_attn_bk"], dtype=np.float32)
    Wv = np.asarray(inputs["L0_attn_Wv"], dtype=np.float32)
    bv = np.asarray(inputs["L0_attn_bv"], dtype=np.float32)
    Wo = np.asarray(inputs["L0_attn_Wo"], dtype=np.float32)
    bo = np.asarray(inputs["L0_attn_bo"], dtype=np.float32)
    gw = np.asarray(inputs["L0_ffn_gate_w"], dtype=np.float32)
    uw = np.asarray(inputs["L0_ffn_up_w"], dtype=np.float32)
    dw = np.asarray(inputs["L0_ffn_down_w"], dtype=np.float32)
    gs = np.asarray(inputs["L0_ffn_gate_s"], dtype=np.float32)
    us = np.asarray(inputs["L0_ffn_up_s"], dtype=np.float32)
    ds = np.asarray(inputs["L0_ffn_down_s"], dtype=np.float32)

    wkT = Wk.T.reshape(NCD, 128, NES, 256).transpose(2, 1, 0, 3)
    wvT = Wv.T.reshape(NCD, 128, NES, 256).transpose(2, 1, 0, 3)
    wkv = np.empty((NES, 128, NCD, 512), dtype=np.float32)
    wkv[:, :, :, 0:256] = wkT
    wkv[:, :, :, 256:512] = wvT
    wqt = np.ascontiguousarray(
        Wq.T.reshape(NCD, 128, 8, 256).transpose(2, 1, 0, 3)).reshape(8 * 128, NCD * 256)
    wot = np.ascontiguousarray(
        Wo.T.reshape(NCD, 128, 8, 256).transpose(2, 1, 0, 3)).reshape(8 * 128, NCD * 256)
    ggT = gw.T.reshape(NCD, 128, NIC, 128).transpose(2, 1, 0, 3)
    uuT = uw.T.reshape(NCD, 128, NIC, 128).transpose(2, 1, 0, 3)
    wgu = np.empty((NIC, 128, NCD, 256), dtype=np.float32)
    wgu[:, :, :, 0:128] = ggT
    wgu[:, :, :, 128:256] = uuT
    wdn = np.ascontiguousarray(
        dw.T.reshape(NIC, 128, NCD, 128).transpose(2, 1, 0, 3)).reshape(16 * 128, NIC * 128)
    bkv = np.empty((128, NES, 512), dtype=np.float32)
    bkv[:, :, 0:256] = np.broadcast_to(bk.reshape(NES, 256), (128, NES, 256))
    bkv[:, :, 256:512] = np.broadcast_to(bv.reshape(NES, 256), (128, NES, 256))

    shared = dict(
        wkvh=np.ascontiguousarray(wkv).reshape(NES * 128, NCD * 512),
        wqh=wqt, woh=wot,
        wguh=np.ascontiguousarray(wgu).reshape(NIC * 128, NCD * 256),
        wdnh=wdn,
        bkvB=np.ascontiguousarray(bkv).reshape(128, NES * 512),
        bqC=np.ascontiguousarray(bq.reshape(16, 128).T),
        boC=np.ascontiguousarray(bo.reshape(16, 128).T),
        gsC=np.ascontiguousarray(gs.reshape(NIC, 128).T),
        usC=np.ascontiguousarray(us.reshape(NIC, 128).T),
        dsC=np.ascontiguousarray(ds.reshape(16, 128).T))

    in_maps = []
    for ci in range(8):
        b, h2 = ci // 2, ci % 2
        xTc = np.ascontiguousarray(x[b, h2 * SO:(h2 + 1) * SO, :].T)
        pos = h2 * SO + np.arange(SO)
        dec = np.exp(-pos / float(SF)).astype(np.float32)
        in_maps.append(dict(
            shared, xT=xTc,
            decC=np.ascontiguousarray(dec.reshape(8, 128).T),
            decB=np.ascontiguousarray(np.broadcast_to(dec, (128, SO)))))

    if "nc" not in _NC_CACHE:
        _NC_CACHE["nc"] = build_program()
    nc = _NC_CACHE["nc"]
    res = run_bass_kernel_spmd(nc, in_maps, core_ids=list(range(8)))

    out = np.empty((4, SF, D), dtype=np.float32)
    for ci in range(8):
        b, h2 = ci // 2, ci % 2
        out[b, h2 * SO:(h2 + 1) * SO, :] = res.results[ci]["y"].T
    return out


# version 25
# speedup vs baseline: 1.1844x; 1.1844x over previous
import contextlib
import numpy as np
import concourse.bass as bass
import concourse.mybir as mybir
from concourse.bass_utils import run_bass_kernel_spmd

F32 = mybir.dt.float32
BF16 = mybir.dt.bfloat16
U8 = mybir.dt.uint8
AF = mybir.ActivationFunctionType
ALU = mybir.AluOpType

D = 2048
NCD = 16
SO = 1024
SF = 2048
H = 16
II = 5632
NIC = 44
NES = 8
EPS_LN = 1e-5
EPS_Z = 1e-6
QLOOK = 7
NQS = QLOOK + 1
RG = [[0, 1], [2, 3], [4, 5], [6, 7]]

ENGINES = ("sync", "gpsimd", "tensor", "vector", "scalar")
DMA_LANES = ("W0", "W1", "W2", "W3", "WB0", "WB1",
             "XW0", "XW1", "XW2", "XW3", "XB0", "XB1", "XB2", "XB3",
             "BK0", "BK1", "CST", "X1S0", "X1S1", "Y0", "Y1", "KVI", "KVO")
SEMS = ("T", "V", "S", "CC") + DMA_LANES
SEM_OWNER = {"T": "tensor", "V": "vector", "S": "scalar"}


class Res:
    __slots__ = ("w", "rs")

    def __init__(self):
        self.w = None
        self.rs = []


class Em:
    def __init__(self):
        self.ops = {e: [] for e in ENGINES}
        self.cnt = {s: 0 for s in SEMS}
        self.lastw = {e: {s: 0 for s in SEMS} for e in ENGINES}
        self.owncnt = {e: {s: 0 for s in SEMS} for e in ENGINES}
        self.lane_res = {lane: Res() for lane in DMA_LANES}

    def emit(self, eng, fn, reads=(), writes=(), waits=(), inc=None):
        wd = {}
        for sem, val in waits:
            wd[sem] = max(wd.get(sem, 0), val)
        for res in reads:
            if res.w:
                sem, val = res.w
                wd[sem] = max(wd.get(sem, 0), val)
        for res in writes:
            if res.w:
                sem, val = res.w
                wd[sem] = max(wd.get(sem, 0), val)
            for sem, val in res.rs:
                wd[sem] = max(wd.get(sem, 0), val)
        ws = []
        for sem, val in wd.items():
            if val <= 0 or val <= self.lastw[eng][sem]:
                continue
            ws.append((sem, val))
            self.lastw[eng][sem] = val
        self.ops[eng].append((fn, ws, inc))
        m = None
        if inc is not None:
            sem, n = inc
            self.cnt[sem] += n
            self.owncnt[eng][sem] += n
            m = (sem, self.cnt[sem])
        if m is not None:
            for res in writes:
                res.w = m
                res.rs = []
            for res in reads:
                res.rs.append(m)
        return m

    def dma(self, eng, out_ap, in_ap, lane, reads=(), writes=(), waits=()):
        return self.emit(eng, lambda e, o=out_ap, i=in_ap: e.dma_start(out=o, in_=i),
                         reads=reads, writes=list(writes) + [self.lane_res[lane]],
                         waits=waits, inc=(lane, 16))


def build_program():
    nc = bass.Bass("TRN2", target_bir_lowering=False)

    xT = nc.dram_tensor("xT", [D, SO], F32, kind="ExternalInput").ap()
    wkvh = nc.dram_tensor("wkvh", [NES * 128, NCD * 512], F32, kind="ExternalInput").ap()
    wqh = nc.dram_tensor("wqh", [8 * 128, NCD * 256], F32, kind="ExternalInput").ap()
    woh = nc.dram_tensor("woh", [8 * 128, NCD * 256], F32, kind="ExternalInput").ap()
    wguh = nc.dram_tensor("wguh", [NIC * 128, NCD * 256], F32, kind="ExternalInput").ap()
    wdnh = nc.dram_tensor("wdnh", [16 * 128, NIC * 128], F32, kind="ExternalInput").ap()
    bkvB_h = nc.dram_tensor("bkvB", [128, NES * 512], F32, kind="ExternalInput").ap()
    bqC_h = nc.dram_tensor("bqC", [128, 16], F32, kind="ExternalInput").ap()
    boC_h = nc.dram_tensor("boC", [128, 16], F32, kind="ExternalInput").ap()
    gsC_h = nc.dram_tensor("gsC", [128, NIC], F32, kind="ExternalInput").ap()
    usC_h = nc.dram_tensor("usC", [128, NIC], F32, kind="ExternalInput").ap()
    dsC_h = nc.dram_tensor("dsC", [128, 16], F32, kind="ExternalInput").ap()
    decC_h = nc.dram_tensor("decC", [128, 8], F32, kind="ExternalInput").ap()
    decB_h = nc.dram_tensor("decB", [128, SO], F32, kind="ExternalInput").ap()
    y = nc.dram_tensor("y", [D, SO], F32, kind="ExternalOutput").ap()
    x1d = nc.dram_tensor("x1d", [D, SO], F32, kind="Internal").ap()
    kvin_t = [nc.dram_tensor("kvin%d" % i, [128, 8 * 129], F32, kind="Internal")
              for i in range(2)]
    kvout_t = [nc.dram_tensor("kvout%d" % i, [128, 8 * 129], F32, kind="Internal")
               for i in range(2)]
    kvin = [t.ap() for t in kvin_t]
    kvout = [t.ap() for t in kvout_t]

    em = Em()
    stack = contextlib.ExitStack()

    AR_BYTES = 107712
    ARENA = nc.alloc_sbuf_tensor("ARENA", [128, AR_BYTES], U8)
    base = nc.lookup_mloc(ARENA).addr

    def at(name, shape, dtype, off):
        return nc.alloc_sbuf_tensor_at(name, shape, dtype, offset=base + off)

    HDN = at("HDN", [128, NIC * 1024], BF16, 0)
    LNX = at("LNX", [128, NCD * 1024], BF16, 0)
    XB16 = at("XB16", [128, NCD * 1024], BF16, 32768)
    KT = at("KT", [128, 8 * 256], BF16, 32768)
    QTB = at("QTB", [128, NQS * 1024], BF16, 36864)
    ZBF = at("ZBF", [128, 2048], F32, 53248)
    XSQ = at("XSQ", [128, 2 * 1024], BF16, 65536)
    KSRF = at("KSRF", [128, H * 128], BF16, 69632)
    KVS = at("KVS", [128, H * 129], F32, 73728)
    KVB = at("KVB", [128, H * 129], BF16, 81984)
    DECB = at("DECB", [128, SO], F32, 86144)
    VTL = at("VTL", [128, 2 * 260], BF16, 90240)
    MSB = at("MSB", [128, SO], BF16, 91328)
    RSB = at("RSB", [128, SO], BF16, 93408)
    VT0B = at("VT0B", [128, SO], BF16, 95488)
    BCS = at("BCS", [128, 2 * 512], F32, 99520)
    VT0 = at("VT0", [128, 1024], F32, 103616)
    assert NQS * 2048 <= 53248 - 36864

    REGB = nc.alloc_sbuf_tensor("REGB", [128, NCD * 1024], BF16)
    WB = nc.alloc_sbuf_tensor("WB", [128, 16384], BF16)
    XW = nc.alloc_sbuf_tensor("XW", [128, 4 * 1024], F32)
    X1W = nc.alloc_sbuf_tensor("X1W", [128, 2 * 1024], F32)
    SILU = nc.alloc_sbuf_tensor("SILU", [128, 1024], F32)
    ONE128 = nc.alloc_sbuf_tensor("ONE128", [128, 128], F32)
    ONE128B = nc.alloc_sbuf_tensor("ONE128B", [128, 128], BF16)
    bqC = nc.alloc_sbuf_tensor("bqC_s", [128, 16], F32)
    boC = nc.alloc_sbuf_tensor("boC_s", [128, 16], F32)
    gsC = nc.alloc_sbuf_tensor("gsC_s", [128, NIC], F32)
    usC = nc.alloc_sbuf_tensor("usC_s", [128, NIC], F32)
    dsC = nc.alloc_sbuf_tensor("dsC_s", [128, 16], F32)
    decC = nc.alloc_sbuf_tensor("decC_s", [128, 8], F32)
    CONSTS = nc.alloc_sbuf_tensor("CONSTS", [128, 4], F32)
    pb0_ = nc.alloc_psum_tensor("pb0", [128, 512], F32)
    pb1_ = nc.alloc_psum_tensor("pb1", [128, 512], F32)
    pb2_ = nc.alloc_psum_tensor("pb2", [128, 512], F32)
    pb3_ = nc.alloc_psum_tensor("pb3", [128, 512], F32)
    pk0_ = nc.alloc_psum_tensor("pk0", [128, 512], F32)
    pk1_ = nc.alloc_psum_tensor("pk1", [128, 512], F32)
    pk2_ = nc.alloc_psum_tensor("pk2", [128, 512], F32)
    pk3_ = nc.alloc_psum_tensor("pk3", [128, 512], F32)

    semmap = {}
    for _sn in SEMS:
        semmap[_sn] = stack.enter_context(nc.semaphore("sem_" + _sn))
    with stack, nc.Block() as block:
        PB = [pb0_, pb1_, pb2_, pb3_]
        PK = [pk0_, pk1_, pk2_, pk3_]
        CEPS5 = CONSTS[:, 0:1]
        CEPS6 = CONSTS[:, 1:2]
        CNEG1 = CONSTS[:, 2:3]
        CNEGH = CONSTS[:, 3:4]

        rPB = [Res() for _ in range(4)]
        rPK = [Res() for _ in range(4)]
        rLNXc = [Res() for _ in range(NCD)]
        rREGBq = [Res() for _ in range(4)]
        rW = [Res() for _ in range(4)]
        rHDN = [Res() for _ in range(NIC)]
        rKT = Res()
        rQTB = [Res() for _ in range(NQS)]
        rLNXq = [Res() for _ in range(4)]
        rZBF = [Res() for _ in range(4)]
        rXB16c = [Res() for _ in range(NCD)]
        rXSQ = [Res(), Res()]
        rKVS = [Res(), Res()]
        rKVB = [Res(), Res()]
        rKSRF = [Res(), Res()]
        rKVIN = [Res(), Res()]
        rDECB = Res()
        rDECC = Res()
        rVTL = [Res(), Res()]
        rMSB = Res()
        rRSB = Res()
        rBCS = [Res(), Res()]
        rVT0a = Res()
        rVT0b = Res()
        rXW = [Res() for _ in range(4)]
        rXBL = [Res() for _ in range(4)]
        rX1W = [Res(), Res()]
        rX1L = [Res(), Res()]
        rX1D = [Res() for _ in range(16)]
        rYL = [Res(), Res()]
        rSILU = Res()
        rCONST = Res()
        rCTAB = Res()

        xw_cur = [0]

        def xw_load(src_ap, reads=(), eng="sync"):
            sl = xw_cur[0] % 4
            xw_cur[0] += 1
            em.dma(eng, XW[:, sl * 1024:(sl + 1) * 1024], src_ap,
                   "XW%d" % sl, reads=reads, writes=[rXW[sl]])
            return XW[:, sl * 1024:(sl + 1) * 1024], rXW[sl]

        def mm(bank_ap, lhsT, rhs, bank_res, reads, start, stop):
            def go(e, b=bank_ap, l=lhsT, r=rhs, st=start, sp=stop):
                return e.matmul(b, l, r, start=st, stop=sp, skip_group_check=True)
            return em.emit("tensor", go, reads=reads, writes=[bank_res], inc=("T", 1))

        def mmgroup(bank_ap, pieces, bank_res, reads):
            n = len(pieces)

            def go(e, b=bank_ap, pcs=pieces):
                last = None
                for i, (l, r) in enumerate(pcs):
                    last = e.matmul(b, l, r, start=(i == 0), stop=(i == n - 1),
                                    skip_group_check=True)
                return last
            return em.emit("tensor", go, reads=reads, writes=[bank_res], inc=("T", 1))

        def mmgroup2(bankA, piecesA, bankB, piecesB, resA, resB, reads):
            nA, nB = len(piecesA), len(piecesB)

            def go(e, a=bankA, b=bankB, pA=piecesA, pB=piecesB):
                for i, (l, r) in enumerate(pA):
                    e.matmul(a, l, r, start=(i == 0), stop=(i == nA - 1),
                             skip_group_check=True)
                last = None
                for i, (l, r) in enumerate(pB):
                    last = e.matmul(b, l, r, start=(i == 0), stop=(i == nB - 1),
                                    skip_group_check=True)
                return last
            return em.emit("tensor", go, reads=reads, writes=[resA, resB],
                           inc=("T", 1))

        def mmmulti(chunks, resl, reads):
            def go(e, cs=chunks):
                last = None
                for b, l, r, st, sp in cs:
                    last = e.matmul(b, l, r, start=st, stop=sp,
                                    skip_group_check=True)
                return last
            return em.emit("tensor", go, reads=reads, writes=resl, inc=("T", 1))

        em.emit("vector", lambda e: e.memset(ONE128[:], 1.0), writes=[rCONST], inc=("V", 1))
        em.emit("vector", lambda e: e.memset(ONE128B[:], 1.0 / 2048.0), writes=[rCONST], inc=("V", 1))
        for slot in range(2):
            for hh in range(2):
                em.emit("vector", lambda e, c=slot * 260 + hh * 130 + 128:
                        e.memset(VTL[:, c:c + 1], 1.0), writes=[rVTL[slot]], inc=("V", 1))
        em.emit("vector", lambda e: e.memset(CEPS5, EPS_LN), writes=[rCONST], inc=("V", 1))
        em.emit("vector", lambda e: e.memset(CEPS6, EPS_Z), writes=[rCONST], inc=("V", 1))
        em.emit("vector", lambda e: e.memset(CNEG1, -1.0), writes=[rCONST], inc=("V", 1))
        em.emit("vector", lambda e: e.memset(CNEGH, -0.5), writes=[rCONST], inc=("V", 1))



        def ln_finalize(banks, bres):
            for th in range(2):
                ts = slice(th * 512, (th + 1) * 512)
                em.emit("vector", lambda e, th=th, ts=ts, banks=banks: e.tensor_copy(
                    out=MSB[:, ts], in_=banks[th][:, 0:512]),
                    reads=[bres[th]], writes=[rMSB], inc=("V", 1))
            for th in range(2):
                ts = slice(th * 512, (th + 1) * 512)
                em.emit("vector", lambda e, th=th, ts=ts: e.tensor_tensor(
                    out=VT0[:, th * 512:(th + 1) * 512], in0=MSB[:, ts],
                    in1=MSB[:, ts], op=ALU.mult),
                    reads=[rMSB], writes=[rVT0a], inc=("V", 1))
            for th in range(2):
                em.emit("vector", lambda e, th=th, banks=banks: e.tensor_tensor(
                    out=SILU[:, th * 512:(th + 1) * 512], in0=banks[2 + th][:, 0:512],
                    in1=VT0[:, th * 512:(th + 1) * 512], op=ALU.subtract),
                    reads=[bres[2 + th], rVT0a], writes=[rVT0b, rSILU], inc=("V", 1))
            for th in range(2):
                em.emit("scalar", lambda e, th=th: e.activation(
                    out=SILU[:, th * 512:(th + 1) * 512],
                    in_=SILU[:, th * 512:(th + 1) * 512], func=AF.Ln,
                    bias=CEPS5), reads=[rCONST], writes=[rVT0b, rSILU], inc=("S", 1))
            for th in range(2):
                ts = slice(th * 512, (th + 1) * 512)
                em.emit("scalar", lambda e, ts=ts, th=th: e.activation(
                    out=RSB[:, ts], in_=SILU[:, th * 512:(th + 1) * 512],
                    func=AF.Exp,
                    scale=CNEGH), reads=[rVT0b, rSILU, rCONST], writes=[rRSB], inc=("S", 1))

        for c in range(NCD):
            sl = c % 2
            em.dma("gpsimd", XB16[:, c * 1024:(c + 1) * 1024],
                   xT[c * 128:(c + 1) * 128, :], "XB%d" % (c % 4),
                   writes=[rXB16c[c], rXBL[c % 4]])
            em.emit("vector", lambda e, c=c, sl=sl: e.tensor_tensor(
                out=XSQ[:, sl * 1024:(sl + 1) * 1024],
                in0=XB16[:, c * 1024:(c + 1) * 1024],
                in1=XB16[:, c * 1024:(c + 1) * 1024], op=ALU.mult),
                reads=[rXB16c[c]], writes=[rXSQ[sl]], inc=("V", 1))
            for th in range(2):
                mm(PB[th][:, 0:512], ONE128B[:],
                   XB16[:, c * 1024 + th * 512: c * 1024 + (th + 1) * 512],
                   rPB[th], reads=[rXB16c[c], rCONST],
                   start=(c == 0), stop=(c == NCD - 1))
            for th in range(2):
                mm(PB[2 + th][:, 0:512], ONE128B[:],
                   XSQ[:, sl * 1024 + th * 512: sl * 1024 + (th + 1) * 512],
                   rPB[2 + th], reads=[rXSQ[sl], rCONST],
                   start=(c == 0), stop=(c == NCD - 1))
        em.dma("gpsimd", WB[:, 0:8192], wkvh[0:128, :], "WB0",
               writes=[rW[0], rW[1]])
        em.dma("gpsimd", WB[:, 8192:16384], wkvh[128:256, :], "WB1",
               writes=[rW[2], rW[3]])
        em.dma("sync", decC[:], decC_h, "CST", writes=[rDECC])
        em.dma("sync", DECB[:], decB_h, "CST", writes=[rDECB])
        em.dma("sync", bqC[:], bqC_h, "CST", writes=[rCTAB])
        em.dma("sync", boC[:], boC_h, "CST", writes=[rCTAB])
        em.dma("sync", gsC[:], gsC_h, "CST", writes=[rCTAB])
        em.dma("sync", usC[:], usC_h, "CST", writes=[rCTAB])
        em.dma("sync", dsC[:], dsC_h, "CST", writes=[rCTAB])
        ln_finalize(PB, rPB)

        for c in range(NCD):
            em.emit("vector", lambda e, c=c: e.tensor_tensor(
                out=VT0B[:], in0=XB16[:, c * 1024:(c + 1) * 1024],
                in1=MSB[:, 0:1024],
                op=ALU.subtract), reads=[rXB16c[c], rMSB],
                writes=[rVT0a, rVT0b], inc=("V", 1))
            em.emit("vector", lambda e, c=c: e.tensor_tensor(
                out=LNX[:, c * 1024:(c + 1) * 1024], in0=VT0B[:],
                in1=RSB[:, 0:1024], op=ALU.mult),
                reads=[rVT0a, rVT0b, rRSB], writes=[rLNXq[c // 4]], inc=("V", 1))

        def cc_trigger(i):
            em.emit("gpsimd", lambda e, i=i: e.collective_compute(
                "AllReduce", ALU.add, replica_groups=RG,
                ins=[kvin_t[i].ap().opt()], outs=[kvout_t[i].ap().opt()]),
                reads=[rKVIN[i]], inc=("CC", 1))

        def bcs_load(s):
            sl2 = s % 2
            em.dma("sync", BCS[:, sl2 * 512:(sl2 + 1) * 512],
                   bkvB_h[:, s * 512:(s + 1) * 512], "BK%d" % sl2,
                   writes=[rBCS[sl2]])

        bcs_load(0)
        for s in range(NES):
            sl2 = s % 2
            if s + 1 < NES:
                bcs_load(s + 1)
            bigs = s % 2
            wslot = WB[:, bigs * 8192:(bigs + 1) * 8192]
            wrs = [rW[2 * bigs], rW[2 * bigs + 1]]
            if s == 6:
                cc_trigger(0)
            if s >= 2:
                em.dma("gpsimd", wslot, wkvh[s * 128:(s + 1) * 128, :],
                       "WB%d" % bigs, writes=wrs)

            def wk(c, w=wslot):
                return w[:, c * 512: c * 512 + 256]

            def wv(c, w=wslot):
                return w[:, c * 512 + 256: (c + 1) * 512]

            def kpieces(tch, c0, c1, wslot=wslot):
                kbank = PB[tch % 2]
                return [(kbank[:, 0:256],
                         LNX[:, c * 1024 + tch * 128: c * 1024 + tch * 128 + 128],
                         wk(c, wslot), c == 0, c == NCD - 1)
                        for c in range(c0, c1)]

            def vpieces(tch, c0, c1, wslot=wslot):
                vbank = PB[2 + tch % 2]
                return [(vbank[:, 0:256],
                         LNX[:, c * 1024 + tch * 128: c * 1024 + tch * 128 + 128],
                         wv(c, wslot), c == 0, c == NCD - 1)
                        for c in range(c0, c1)]

            def k_evac(tch, sl2=sl2):
                kbank = PB[tch % 2]
                em.emit("vector", lambda e, kbank=kbank, sl2=sl2: e.tensor_tensor(
                    out=VT0[:, 0:256], in0=kbank[:, 0:256],
                    in1=BCS[:, sl2 * 512: sl2 * 512 + 256],
                    op=ALU.add), reads=[rPB[tch % 2], rBCS[sl2]],
                    writes=[rVT0a], inc=("V", 1))
                em.emit("scalar", lambda e, tch=tch: e.activation(
                    out=KT[:, tch * 256:(tch + 1) * 256], in_=VT0[:, 0:256],
                    func=AF.Relu, scale=decC[:, tch:tch + 1]),
                    reads=[rVT0a, rDECC], writes=[rKT], inc=("S", 1))

            def v_evac(tch, sl2=sl2):
                vslot = tch % 2
                vbank = PB[2 + tch % 2]
                em.emit("vector", lambda e, vbank=vbank, vbase=vslot * 260, sl2=sl2: e.tensor_tensor(
                    out=VTL[:, vbase:vbase + 260].rearrange(
                        "p (h e) -> p h e", h=2)[:, :, 0:128],
                    in0=vbank[:, 0:256].rearrange("p (h e) -> p h e", h=2),
                    in1=BCS[:, sl2 * 512 + 256: sl2 * 512 + 512].rearrange(
                        "p (h e) -> p h e", h=2),
                    op=ALU.add), reads=[rPB[2 + tch % 2], rBCS[sl2]],
                    writes=[rVTL[vslot]], inc=("V", 1))

            def kv_mms(tch, s=s):
                vbase = (tch % 2) * 260
                for hh in range(2):
                    hg = s * 2 + hh
                    mm(PK[hg % 4][:, 0:129],
                       KT[:, tch * 256 + hh * 128: tch * 256 + hh * 128 + 128],
                       VTL[:, vbase + hh * 130: vbase + hh * 130 + 129],
                       rPK[hg % 4], reads=[rKT, rVTL[tch % 2]],
                       start=(tch == 0), stop=(tch == 7))

            if s == 0:
                for q in range(4):
                    for tch in (0, 1):
                        mmmulti(kpieces(tch, 4 * q, 4 * q + 4),
                                [rPB[tch % 2]], reads=[rLNXq[q]] + wrs)
                    for tch in (0, 1):
                        mmmulti(vpieces(tch, 4 * q, 4 * q + 4),
                                [rPB[2 + tch % 2]], reads=[rLNXq[q]] + wrs)
                for tch in (0, 1):
                    k_evac(tch)
                for tch in (0, 1):
                    v_evac(tch)
                for tch in range(2, 8):
                    mmmulti(kpieces(tch, 0, NCD), [rPB[tch % 2]],
                            reads=rLNXq + wrs)
                    k_evac(tch)
                kv_mms(0)
                for tch in range(2, 8):
                    mmmulti(vpieces(tch, 0, NCD), [rPB[2 + tch % 2]],
                            reads=rLNXq + wrs)
                    v_evac(tch)
                    kv_mms(tch - 1)
                kv_mms(7)
            else:
                for tch in range(8):
                    mmmulti(kpieces(tch, 0, NCD), [rPB[tch % 2]],
                            reads=rLNXq + wrs)
                    k_evac(tch)
                for tch in range(8):
                    mmmulti(vpieces(tch, 0, NCD), [rPB[2 + tch % 2]],
                            reads=rLNXq + wrs)
                    v_evac(tch)
                    if tch > 0:
                        kv_mms(tch - 1)
                kv_mms(7)
            for hh in range(2):
                hg = s * 2 + hh
                cs = slice(hg * 129, hg * 129 + 129)
                em.emit("vector", lambda e, hg=hg, cs=cs: e.tensor_copy(
                    out=KVS[:, cs], in_=PK[hg % 4][:, 0:129]),
                    reads=[rPK[hg % 4]], writes=[rKVS[s // 4]], inc=("V", 1))
            if s == 3:
                em.dma("sync", kvin[0], KVS[:, 0:1032], "KVI",
                       reads=[rKVS[0]], writes=[rKVIN[0]])

        def q_dma(g):
            sl = g % 4
            em.dma("gpsimd", WB[:, sl * 4096:(sl + 1) * 4096],
                   wqh[g * 128:(g + 1) * 128, :], "W%d" % sl, writes=[rW[sl]])

        q_dma(0)
        q_dma(1)
        q_dma(2)

        em.dma("sync", kvin[1], KVS[:, 1032:2064], "KVI",
               reads=[rKVS[1]], writes=[rKVIN[1]])
        cc_trigger(1)
        em.dma("scalar", KVS[:, 0:1032], kvout[0], "KVO", waits=[("CC", 1)],
               writes=[rKVS[0]])
        em.dma("sync", KVS[:, 1032:2064], kvout[1], "KVO", waits=[("CC", 2)],
               writes=[rKVS[1]])

        def q_stage(h):
            if h % 2 == 0 and h >= 6:
                q_dma(h // 2)
            sl = (h // 2) % 4
            qs = h % NQS
            qb = 2 * (h % 2)
            pcs = [[(WB[:, sl * 4096 + c * 256 + (h % 2) * 128:
                        sl * 4096 + c * 256 + (h % 2) * 128 + 128],
                     LNX[:, c * 1024 + th * 512: c * 1024 + th * 512 + 512])
                    for c in range(NCD)] for th in range(2)]
            mmgroup2(PB[qb][:, 0:512], pcs[0], PB[qb + 1][:, 0:512], pcs[1],
                     rPB[qb], rPB[qb + 1], reads=[rW[sl]] + rLNXq)
            for th in range(2):
                vt = rVT0a if th == 0 else rVT0b
                em.emit("scalar", lambda e, th=th, h=h, qb=qb: e.activation(
                    out=VT0[:, th * 512:(th + 1) * 512], in_=PB[qb + th][:, 0:512],
                    func=AF.Relu, bias=bqC[:, h:h + 1]),
                    reads=[rPB[qb + th], rCTAB], writes=[vt], inc=("S", 1))
                em.emit("vector", lambda e, th=th, qs=qs: e.tensor_tensor(
                    out=QTB[:, qs * 1024 + th * 512: qs * 1024 + th * 512 + 512],
                    in0=VT0[:, th * 512:(th + 1) * 512],
                    in1=DECB[:, th * 512:(th + 1) * 512], op=ALU.mult),
                    reads=[vt, rDECB], writes=[rQTB[qs]], inc=("V", 1))

        def attn_stage(h):
            qs = h % NQS
            hf = h // 8
            for th in range(2):
                qsl = slice(qs * 1024 + th * 512, qs * 1024 + th * 512 + 512)
                mm(PK[2 + th][:, 0:512], KSRF[:, h * 128:(h + 1) * 128],
                   QTB[:, qsl], rPK[2 + th],
                   reads=[rKSRF[hf], rQTB[qs]], start=True, stop=True)
            for th in range(2):
                zs = th + 2 * (h % 2)
                em.emit("scalar", lambda e, th=th, zs=zs: e.activation(
                    out=ZBF[:, zs * 512:(zs + 1) * 512], in_=PK[2 + th][:, 0:512],
                    func=AF.Ln, bias=CEPS6),
                    reads=[rPK[2 + th], rCONST], writes=[rZBF[zs]], inc=("S", 1))
            for th in range(2):
                zs = th + 2 * (h % 2)
                em.emit("scalar", lambda e, zs=zs: e.activation(
                    out=ZBF[:, zs * 512:(zs + 1) * 512],
                    in_=ZBF[:, zs * 512:(zs + 1) * 512],
                    func=AF.Exp, scale=CNEG1),
                    reads=[rCONST], writes=[rZBF[zs]], inc=("S", 1))
            for th in range(2):
                qsl = slice(qs * 1024 + th * 512, qs * 1024 + th * 512 + 512)
                mm(PK[th][:, 0:512], KVB[:, h * 129: h * 129 + 128], QTB[:, qsl],
                   rPK[th], reads=[rKVB[hf], rQTB[qs]], start=True, stop=True)
            for th in range(2):
                zs = th + 2 * (h % 2)
                em.emit("vector", lambda e, th=th, h=h, zs=zs: e.tensor_tensor(
                    out=REGB[:, h * 1024 + th * 512: h * 1024 + th * 512 + 512],
                    in0=PK[th][:, 0:512], in1=ZBF[:, zs * 512:(zs + 1) * 512],
                    op=ALU.mult), reads=[rPK[th], rZBF[zs]],
                    writes=[rREGBq[h // 4]], inc=("V", 1))

        def kv_finish(hf):
            em.emit("scalar", lambda e, hf=hf: e.activation(
                out=KVB[:, hf * 1032:(hf + 1) * 1032],
                in_=KVS[:, hf * 1032:(hf + 1) * 1032], func=AF.Identity),
                reads=[rKVS[hf]], writes=[rKVB[hf]], inc=("S", 1))
            for h in range(hf * 8, hf * 8 + 8):
                em.emit("vector", lambda e, h=h: e.tensor_scalar(
                    out=KSRF[:, h * 128:(h + 1) * 128], in0=ONE128[:],
                    scalar1=KVS[:, h * 129 + 128: h * 129 + 129],
                    scalar2=None, op0=ALU.mult),
                    reads=[rKVS[hf], rCONST], writes=[rKSRF[hf]], inc=("V", 1))

        kv_finish(0)
        for h in range(QLOOK + 1):
            q_stage(h)
        for h in range(H):
            attn_stage(h)
            if h == 3:
                kv_finish(1)
            if h + QLOOK + 1 < H:
                q_stage(h + QLOOK + 1)

        def x1_stats(j):
            x1sl = j % 2
            for th in range(2):
                mm(PK[th][:, 0:512], ONE128B[:],
                   LNX[:, j * 1024 + th * 512: j * 1024 + (th + 1) * 512],
                   rPK[th], reads=[rLNXc[j], rCONST],
                   start=(j == 0), stop=(j == NCD - 1))
            for th in range(2):
                mm(PK[2 + th][:, 0:512], ONE128B[:],
                   XSQ[:, x1sl * 1024 + th * 512: x1sl * 1024 + (th + 1) * 512],
                   rPK[2 + th], reads=[rXSQ[x1sl], rCONST],
                   start=(j == 0), stop=(j == NCD - 1))

        for j in range(NCD):
            if j % 2 == 0:
                g2 = j // 2
                sl = g2 % 2
                em.dma("gpsimd", WB[:, sl * 4096:(sl + 1) * 4096],
                       woh[g2 * 128:(g2 + 1) * 128, :], "W%d" % sl,
                       writes=[rW[sl]])
            sl = (j // 2) % 2
            ob = 2 * (j % 2)
            xwap, rxw = xw_load(xT[j * 128:(j + 1) * 128, :], eng="scalar")
            pcs = [[(WB[:, sl * 4096 + c * 256 + (j % 2) * 128:
                        sl * 4096 + c * 256 + (j % 2) * 128 + 128],
                     REGB[:, c * 1024 + th * 512: c * 1024 + th * 512 + 512])
                    for c in range(NCD)] for th in range(2)]
            mmgroup2(PB[ob][:, 0:512], pcs[0], PB[ob + 1][:, 0:512], pcs[1],
                     rPB[ob], rPB[ob + 1], reads=[rW[sl]] + rREGBq)
            if j > 0:
                x1_stats(j - 1)
            x1sl = j % 2
            for th in range(2):
                vt = rVT0a if th == 0 else rVT0b
                em.emit("scalar", lambda e, th=th, j=j, ob=ob: e.activation(
                    out=VT0[:, th * 512:(th + 1) * 512], in_=PB[ob + th][:, 0:512],
                    func=AF.Identity, bias=boC[:, j:j + 1]),
                    reads=[rPB[ob + th], rCTAB], writes=[vt], inc=("S", 1))
                em.emit("vector", lambda e, th=th, x1sl=x1sl, xwap=xwap: e.tensor_tensor(
                    out=X1W[:, x1sl * 1024 + th * 512: x1sl * 1024 + (th + 1) * 512],
                    in0=VT0[:, th * 512:(th + 1) * 512],
                    in1=xwap[:, th * 512:(th + 1) * 512], op=ALU.add),
                    reads=[vt, rxw], writes=[rX1W[x1sl]], inc=("V", 1))
            em.dma("sync", x1d[j * 128:(j + 1) * 128, :],
                   X1W[:, x1sl * 1024:(x1sl + 1) * 1024], "X1S%d" % x1sl,
                   reads=[rX1W[x1sl]], writes=[rX1L[x1sl], rX1D[j]])
            em.emit("vector", lambda e, j=j, x1sl=x1sl: e.tensor_copy(
                out=LNX[:, j * 1024:(j + 1) * 1024],
                in_=X1W[:, x1sl * 1024:(x1sl + 1) * 1024]),
                reads=[rX1W[x1sl]], writes=[rLNXc[j]], inc=("V", 1))
            em.emit("vector", lambda e, j=j, x1sl=x1sl: e.tensor_tensor(
                out=XSQ[:, x1sl * 1024:(x1sl + 1) * 1024],
                in0=LNX[:, j * 1024:(j + 1) * 1024],
                in1=LNX[:, j * 1024:(j + 1) * 1024], op=ALU.mult),
                reads=[rLNXc[j]], writes=[rXSQ[x1sl]], inc=("V", 1))
        x1_stats(NCD - 1)
        ln_finalize(PK, rPK)

        for c in range(NCD):
            em.emit("vector", lambda e, c=c: e.tensor_tensor(
                out=VT0B[:], in0=LNX[:, c * 1024:(c + 1) * 1024],
                in1=MSB[:, 0:1024],
                op=ALU.subtract), reads=[rLNXc[c], rMSB],
                writes=[rVT0a, rVT0b], inc=("V", 1))
            em.emit("vector", lambda e, c=c: e.tensor_tensor(
                out=REGB[:, c * 1024:(c + 1) * 1024], in0=VT0B[:],
                in1=RSB[:, 0:1024], op=ALU.mult),
                reads=[rVT0a, rVT0b, rRSB], writes=[rREGBq[c // 4]], inc=("V", 1))

        def wgu_dma(ic):
            sl = 2 + ic % 2
            em.dma("gpsimd", WB[:, sl * 4096:(sl + 1) * 4096],
                   wguh[ic * 128:(ic + 1) * 128, :], "W%d" % sl,
                   writes=[rW[sl]])

        wgu_dma(0)
        wgu_dma(1)
        em.dma("gpsimd", WB[:, 0:5632], wdnh[0:128, :], "WB0",
               writes=[rW[0], rW[1]])
        for ic in range(NIC):
            sl = 2 + ic % 2
            if ic >= 2:
                wgu_dma(ic)
            a = 2 * (ic % 2)
            G = [PB[a], PB[a + 1]]
            U = [PK[a], PK[a + 1]]

            def gu_chunks(c0, c1):
                chunks = []
                for c in range(c0, c1):
                    gap = WB[:, sl * 4096 + c * 256: sl * 4096 + c * 256 + 128]
                    uap = WB[:, sl * 4096 + c * 256 + 128: sl * 4096 + (c + 1) * 256]
                    st, sp = (c == 0), (c == NCD - 1)
                    for th in range(2):
                        chunks.append((G[th][:, 0:512], gap,
                                       REGB[:, c * 1024 + th * 512: c * 1024 + (th + 1) * 512],
                                       st, sp))
                    for th in range(2):
                        chunks.append((U[th][:, 0:512], uap,
                                       REGB[:, c * 1024 + th * 512: c * 1024 + (th + 1) * 512],
                                       st, sp))
                return chunks
            bres = [rPB[a], rPB[a + 1], rPK[a], rPK[a + 1]]
            if ic <= 1:
                for q in range(4):
                    mmmulti(gu_chunks(4 * q, 4 * q + 4), bres,
                            reads=[rW[sl], rREGBq[q]])
            else:
                mmmulti(gu_chunks(0, NCD), bres, reads=[rW[sl]] + rREGBq)
            for th in range(2):
                em.emit("scalar", lambda e, th=th, ic=ic, G=G: e.activation(
                    out=SILU[:, th * 512:(th + 1) * 512], in_=G[th][:, 0:512],
                    func=AF.Silu, scale=gsC[:, ic:ic + 1]),
                    reads=[rPB[a + th], rCTAB], writes=[rSILU], inc=("S", 1))
            for th in range(2):
                em.emit("vector", lambda e, th=th, ic=ic, U=U: e.scalar_tensor_tensor(
                    out=HDN[:, ic * 1024 + th * 512: ic * 1024 + (th + 1) * 512],
                    in0=U[th][:, 0:512], scalar=usC[:, ic:ic + 1],
                    in1=SILU[:, th * 512:(th + 1) * 512],
                    op0=ALU.mult, op1=ALU.mult),
                    reads=[rPK[a + th], rSILU, rCTAB], writes=[rHDN[ic]],
                    inc=("V", 1))

        em.dma("gpsimd", WB[:, 8192: 8192 + 5632], wdnh[128:256, :], "WB1",
               writes=[rW[2], rW[3]])
        BPAIRS = [(PB[0], PB[1], rPB[0], rPB[1]), (PB[2], PB[3], rPB[2], rPB[3]),
                  (PK[0], PK[1], rPK[0], rPK[1]), (PK[2], PK[3], rPK[2], rPK[3])]
        for j in range(NCD):
            bigs = j % 2
            if j >= 2:
                em.dma("gpsimd", WB[:, bigs * 8192: bigs * 8192 + 5632],
                       wdnh[j * 128:(j + 1) * 128, :], "WB%d" % bigs,
                       writes=[rW[2 * bigs], rW[2 * bigs + 1]])
            wrs = [rW[2 * bigs], rW[2 * bigs + 1]]
            xwap, rxw = xw_load(x1d[j * 128:(j + 1) * 128, :], reads=[rX1D[j]])
            b0, b1, rb0, rb1 = BPAIRS[j % 4]
            chunks = []
            for ic in range(NIC):
                dap = WB[:, bigs * 8192 + ic * 128: bigs * 8192 + (ic + 1) * 128]
                st, sp = (ic == 0), (ic == NIC - 1)
                chunks.append((b0[:, 0:512], dap, HDN[:, ic * 1024: ic * 1024 + 512],
                               st, sp))
                chunks.append((b1[:, 0:512], dap, HDN[:, ic * 1024 + 512: (ic + 1) * 1024],
                               st, sp))
            mmmulti(chunks, [rb0, rb1], reads=wrs + rHDN)
            x1sl = j % 2
            for th in range(2):
                bb = b0 if th == 0 else b1
                rb = rb0 if th == 0 else rb1
                em.emit("vector", lambda e, th=th, j=j, x1sl=x1sl, bb=bb, xwap=xwap: e.scalar_tensor_tensor(
                    out=X1W[:, x1sl * 1024 + th * 512: x1sl * 1024 + (th + 1) * 512],
                    in0=bb[:, 0:512], scalar=dsC[:, j:j + 1],
                    in1=xwap[:, th * 512:(th + 1) * 512],
                    op0=ALU.mult, op1=ALU.add),
                    reads=[rb, rxw, rCTAB], writes=[rX1W[x1sl]], inc=("V", 1))
            em.dma("sync", y[j * 128:(j + 1) * 128, :],
                   X1W[:, x1sl * 1024:(x1sl + 1) * 1024], "Y%d" % (j % 2),
                   reads=[rX1W[x1sl]], writes=[rYL[j % 2]])

        fin = [(lane, em.cnt[lane]) for lane in ("Y0", "Y1") if em.cnt[lane] > 0]

        def emit_engine(eng_name):
            def go(e):
                for fn, ws, inc in em.ops[eng_name]:
                    for sem, val in ws:
                        e.wait_ge(semmap[sem], val)
                    inst = fn(e)
                    if inc is not None:
                        inst.then_inc(semmap[inc[0]], inc[1])
                if eng_name == "sync":
                    for lane, val in fin:
                        e.wait_ge(semmap[lane], val)
            return go

        block.sync(emit_engine("sync"))
        block.gpsimd(emit_engine("gpsimd"))
        block.tensor(emit_engine("tensor"))
        block.vector(emit_engine("vector"))
        block.scalar(emit_engine("scalar"))

    return nc


_NC_CACHE = {}


def kernel(**inputs):
    x = np.asarray(inputs["L0_attn_x"], dtype=np.float32)
    Wq = np.asarray(inputs["L0_attn_Wq"], dtype=np.float32)
    bq = np.asarray(inputs["L0_attn_bq"], dtype=np.float32)
    Wk = np.asarray(inputs["L0_attn_Wk"], dtype=np.float32)
    bk = np.asarray(inputs["L0_attn_bk"], dtype=np.float32)
    Wv = np.asarray(inputs["L0_attn_Wv"], dtype=np.float32)
    bv = np.asarray(inputs["L0_attn_bv"], dtype=np.float32)
    Wo = np.asarray(inputs["L0_attn_Wo"], dtype=np.float32)
    bo = np.asarray(inputs["L0_attn_bo"], dtype=np.float32)
    gw = np.asarray(inputs["L0_ffn_gate_w"], dtype=np.float32)
    uw = np.asarray(inputs["L0_ffn_up_w"], dtype=np.float32)
    dw = np.asarray(inputs["L0_ffn_down_w"], dtype=np.float32)
    gs = np.asarray(inputs["L0_ffn_gate_s"], dtype=np.float32)
    us = np.asarray(inputs["L0_ffn_up_s"], dtype=np.float32)
    ds = np.asarray(inputs["L0_ffn_down_s"], dtype=np.float32)

    wkT = Wk.T.reshape(NCD, 128, NES, 256).transpose(2, 1, 0, 3)
    wvT = Wv.T.reshape(NCD, 128, NES, 256).transpose(2, 1, 0, 3)
    wkv = np.empty((NES, 128, NCD, 512), dtype=np.float32)
    wkv[:, :, :, 0:256] = wkT
    wkv[:, :, :, 256:512] = wvT
    wqt = np.ascontiguousarray(
        Wq.T.reshape(NCD, 128, 8, 256).transpose(2, 1, 0, 3)).reshape(8 * 128, NCD * 256)
    wot = np.ascontiguousarray(
        Wo.T.reshape(NCD, 128, 8, 256).transpose(2, 1, 0, 3)).reshape(8 * 128, NCD * 256)
    ggT = gw.T.reshape(NCD, 128, NIC, 128).transpose(2, 1, 0, 3)
    uuT = uw.T.reshape(NCD, 128, NIC, 128).transpose(2, 1, 0, 3)
    wgu = np.empty((NIC, 128, NCD, 256), dtype=np.float32)
    wgu[:, :, :, 0:128] = ggT
    wgu[:, :, :, 128:256] = uuT
    wdn = np.ascontiguousarray(
        dw.T.reshape(NIC, 128, NCD, 128).transpose(2, 1, 0, 3)).reshape(16 * 128, NIC * 128)
    bkv = np.empty((128, NES, 512), dtype=np.float32)
    bkv[:, :, 0:256] = np.broadcast_to(bk.reshape(NES, 256), (128, NES, 256))
    bkv[:, :, 256:512] = np.broadcast_to(bv.reshape(NES, 256), (128, NES, 256))

    shared = dict(
        wkvh=np.ascontiguousarray(wkv).reshape(NES * 128, NCD * 512),
        wqh=wqt, woh=wot,
        wguh=np.ascontiguousarray(wgu).reshape(NIC * 128, NCD * 256),
        wdnh=wdn,
        bkvB=np.ascontiguousarray(bkv).reshape(128, NES * 512),
        bqC=np.ascontiguousarray(bq.reshape(16, 128).T),
        boC=np.ascontiguousarray(bo.reshape(16, 128).T),
        gsC=np.ascontiguousarray(gs.reshape(NIC, 128).T),
        usC=np.ascontiguousarray(us.reshape(NIC, 128).T),
        dsC=np.ascontiguousarray(ds.reshape(16, 128).T))

    in_maps = []
    for ci in range(8):
        b, h2 = ci // 2, ci % 2
        xTc = np.ascontiguousarray(x[b, h2 * SO:(h2 + 1) * SO, :].T)
        pos = h2 * SO + np.arange(SO)
        dec = np.exp(-pos / float(SF)).astype(np.float32)
        in_maps.append(dict(
            shared, xT=xTc,
            decC=np.ascontiguousarray(dec.reshape(8, 128).T),
            decB=np.ascontiguousarray(np.broadcast_to(dec, (128, SO)))))

    if "nc" not in _NC_CACHE:
        _NC_CACHE["nc"] = build_program()
    nc = _NC_CACHE["nc"]
    res = run_bass_kernel_spmd(nc, in_maps, core_ids=list(range(8)))

    out = np.empty((4, SF, D), dtype=np.float32)
    for ci in range(8):
        b, h2 = ci // 2, ci % 2
        out[b, h2 * SO:(h2 + 1) * SO, :] = res.results[ci]["y"].T
    return out


# version 26
# speedup vs baseline: 1.2001x; 1.0133x over previous
import contextlib
import numpy as np
import concourse.bass as bass
import concourse.mybir as mybir
from concourse.bass_utils import run_bass_kernel_spmd

F32 = mybir.dt.float32
BF16 = mybir.dt.bfloat16
U8 = mybir.dt.uint8
AF = mybir.ActivationFunctionType
ALU = mybir.AluOpType

D = 2048
NCD = 16
SO = 1024
SF = 2048
H = 16
II = 5632
NIC = 44
NES = 8
EPS_LN = 1e-5
EPS_Z = 1e-6
QLOOK = 7
NQS = QLOOK + 1
RG = [[0, 1], [2, 3], [4, 5], [6, 7]]

ENGINES = ("sync", "gpsimd", "tensor", "vector", "scalar")
DMA_LANES = ("W0", "W1", "W2", "W3", "WB0", "WB1",
             "XW0", "XW1", "XW2", "XW3", "XB0", "XB1", "XB2", "XB3",
             "BK0", "BK1", "CST", "X1S0", "X1S1", "Y0", "Y1", "KVI", "KVO")
SEMS = ("T", "V", "S", "CC") + DMA_LANES
SEM_OWNER = {"T": "tensor", "V": "vector", "S": "scalar"}


class Res:
    __slots__ = ("w", "rs")

    def __init__(self):
        self.w = None
        self.rs = []


class Em:
    def __init__(self):
        self.ops = {e: [] for e in ENGINES}
        self.cnt = {s: 0 for s in SEMS}
        self.lastw = {e: {s: 0 for s in SEMS} for e in ENGINES}
        self.owncnt = {e: {s: 0 for s in SEMS} for e in ENGINES}
        self.lane_res = {lane: Res() for lane in DMA_LANES}

    def emit(self, eng, fn, reads=(), writes=(), waits=(), inc=None):
        wd = {}
        for sem, val in waits:
            wd[sem] = max(wd.get(sem, 0), val)
        for res in reads:
            if res.w:
                sem, val = res.w
                wd[sem] = max(wd.get(sem, 0), val)
        for res in writes:
            if res.w:
                sem, val = res.w
                wd[sem] = max(wd.get(sem, 0), val)
            for sem, val in res.rs:
                wd[sem] = max(wd.get(sem, 0), val)
        ws = []
        for sem, val in wd.items():
            if val <= 0 or val <= self.lastw[eng][sem]:
                continue
            ws.append((sem, val))
            self.lastw[eng][sem] = val
        self.ops[eng].append((fn, ws, inc))
        m = None
        if inc is not None:
            sem, n = inc
            self.cnt[sem] += n
            self.owncnt[eng][sem] += n
            m = (sem, self.cnt[sem])
        if m is not None:
            for res in writes:
                res.w = m
                res.rs = []
            for res in reads:
                res.rs.append(m)
        return m

    def dma(self, eng, out_ap, in_ap, lane, reads=(), writes=(), waits=()):
        return self.emit(eng, lambda e, o=out_ap, i=in_ap: e.dma_start(out=o, in_=i),
                         reads=reads, writes=list(writes) + [self.lane_res[lane]],
                         waits=waits, inc=(lane, 16))


def build_program():
    nc = bass.Bass("TRN2", target_bir_lowering=False)

    xT = nc.dram_tensor("xT", [D, SO], F32, kind="ExternalInput").ap()
    wkvh = nc.dram_tensor("wkvh", [NES * 128, NCD * 512], F32, kind="ExternalInput").ap()
    wqh = nc.dram_tensor("wqh", [8 * 128, NCD * 256], F32, kind="ExternalInput").ap()
    woh = nc.dram_tensor("woh", [8 * 128, NCD * 256], F32, kind="ExternalInput").ap()
    wguh = nc.dram_tensor("wguh", [NIC * 128, NCD * 256], F32, kind="ExternalInput").ap()
    wdnh = nc.dram_tensor("wdnh", [16 * 128, NIC * 128], F32, kind="ExternalInput").ap()
    bkvB_h = nc.dram_tensor("bkvB", [128, NES * 512], F32, kind="ExternalInput").ap()
    bqC_h = nc.dram_tensor("bqC", [128, 16], F32, kind="ExternalInput").ap()
    boC_h = nc.dram_tensor("boC", [128, 16], F32, kind="ExternalInput").ap()
    gsC_h = nc.dram_tensor("gsC", [128, NIC], F32, kind="ExternalInput").ap()
    usC_h = nc.dram_tensor("usC", [128, NIC], F32, kind="ExternalInput").ap()
    dsC_h = nc.dram_tensor("dsC", [128, 16], F32, kind="ExternalInput").ap()
    decC_h = nc.dram_tensor("decC", [128, 8], F32, kind="ExternalInput").ap()
    decB_h = nc.dram_tensor("decB", [128, SO], F32, kind="ExternalInput").ap()
    y = nc.dram_tensor("y", [D, SO], F32, kind="ExternalOutput").ap()
    x1d = nc.dram_tensor("x1d", [D, SO], F32, kind="Internal").ap()
    kvin_t = [nc.dram_tensor("kvin%d" % i, [128, 8 * 129], F32, kind="Internal")
              for i in range(2)]
    kvout_t = [nc.dram_tensor("kvout%d" % i, [128, 8 * 129], F32, kind="Internal")
               for i in range(2)]
    kvin = [t.ap() for t in kvin_t]
    kvout = [t.ap() for t in kvout_t]

    em = Em()
    stack = contextlib.ExitStack()

    AR_BYTES = 111808
    ARENA = nc.alloc_sbuf_tensor("ARENA", [128, AR_BYTES], U8)
    base = nc.lookup_mloc(ARENA).addr

    def at(name, shape, dtype, off):
        return nc.alloc_sbuf_tensor_at(name, shape, dtype, offset=base + off)

    HDN = at("HDN", [128, NIC * 1024], BF16, 0)
    LNX = at("LNX", [128, NCD * 1024], BF16, 0)
    XB16 = at("XB16", [128, NCD * 1024], BF16, 32768)
    KT = at("KT", [128, 8 * 256], BF16, 32768)
    QTB = at("QTB", [128, NQS * 1024], BF16, 36864)
    ZBF = at("ZBF", [128, 2048], F32, 53248)
    XSQ = at("XSQ", [128, 2 * 1024], BF16, 65536)
    KSRF = at("KSRF", [128, H * 128], BF16, 69632)
    KVS = at("KVS", [128, H * 129], F32, 73728)
    KVB = at("KVB", [128, H * 129], BF16, 81984)
    DECB = at("DECB", [128, SO], F32, 86144)
    VTL = at("VTL", [128, 2 * 260], BF16, 90240)
    MSB = at("MSB", [128, 2 * SO], BF16, 91328)
    RSB = at("RSB", [128, 2 * SO], BF16, 95424)
    VT0B = at("VT0B", [128, 2 * SO], BF16, 99520)
    BCS = at("BCS", [128, 2 * 512], F32, 103616)
    VT0 = at("VT0", [128, 1024], F32, 107712)
    assert NQS * 2048 <= 53248 - 36864

    REGB = nc.alloc_sbuf_tensor("REGB", [128, NCD * 1024], BF16)
    WB = nc.alloc_sbuf_tensor("WB", [128, 16384], BF16)
    XW = nc.alloc_sbuf_tensor("XW", [128, 4 * 1024], F32)
    X1W = nc.alloc_sbuf_tensor("X1W", [128, 2 * 1024], F32)
    SILU = nc.alloc_sbuf_tensor("SILU", [128, 1024], F32)
    ONE128 = nc.alloc_sbuf_tensor("ONE128", [128, 128], F32)
    ONE128B = nc.alloc_sbuf_tensor("ONE128B", [128, 128], BF16)
    bqC = nc.alloc_sbuf_tensor("bqC_s", [128, 16], F32)
    boC = nc.alloc_sbuf_tensor("boC_s", [128, 16], F32)
    gsC = nc.alloc_sbuf_tensor("gsC_s", [128, NIC], F32)
    usC = nc.alloc_sbuf_tensor("usC_s", [128, NIC], F32)
    dsC = nc.alloc_sbuf_tensor("dsC_s", [128, 16], F32)
    decC = nc.alloc_sbuf_tensor("decC_s", [128, 8], F32)
    CONSTS = nc.alloc_sbuf_tensor("CONSTS", [128, 4], F32)
    pb0_ = nc.alloc_psum_tensor("pb0", [128, 512], F32)
    pb1_ = nc.alloc_psum_tensor("pb1", [128, 512], F32)
    pb2_ = nc.alloc_psum_tensor("pb2", [128, 512], F32)
    pb3_ = nc.alloc_psum_tensor("pb3", [128, 512], F32)
    pk0_ = nc.alloc_psum_tensor("pk0", [128, 512], F32)
    pk1_ = nc.alloc_psum_tensor("pk1", [128, 512], F32)
    pk2_ = nc.alloc_psum_tensor("pk2", [128, 512], F32)
    pk3_ = nc.alloc_psum_tensor("pk3", [128, 512], F32)

    semmap = {}
    for _sn in SEMS:
        semmap[_sn] = stack.enter_context(nc.semaphore("sem_" + _sn))
    with stack, nc.Block() as block:
        PB = [pb0_, pb1_, pb2_, pb3_]
        PK = [pk0_, pk1_, pk2_, pk3_]
        CEPS5 = CONSTS[:, 0:1]
        CEPS6 = CONSTS[:, 1:2]
        CNEG1 = CONSTS[:, 2:3]
        CNEGH = CONSTS[:, 3:4]

        rPB = [Res() for _ in range(4)]
        rPK = [Res() for _ in range(4)]
        rLNXc = [Res() for _ in range(NCD)]
        rREGBq = [Res() for _ in range(4)]
        rW = [Res() for _ in range(4)]
        rHDN = [Res() for _ in range(NIC)]
        rKT = Res()
        rQTB = [Res() for _ in range(NQS)]
        rLNXq = [Res() for _ in range(4)]
        rZBF = [Res() for _ in range(4)]
        rXB16c = [Res() for _ in range(NCD)]
        rXSQ = [Res(), Res()]
        rKVS = [Res(), Res()]
        rKVB = [Res(), Res()]
        rKSRF = [Res(), Res()]
        rKVIN = [Res(), Res()]
        rDECB = Res()
        rDECC = Res()
        rVTL = [Res(), Res()]
        rMSB = Res()
        rRSB = Res()
        rBCS = [Res(), Res()]
        rVT0a = Res()
        rVT0b = Res()
        rXW = [Res() for _ in range(4)]
        rXBL = [Res() for _ in range(4)]
        rX1W = [Res(), Res()]
        rX1L = [Res(), Res()]
        rX1D = [Res() for _ in range(16)]
        rYL = [Res(), Res()]
        rSILU = Res()
        rCONST = Res()
        rCTAB = Res()

        xw_cur = [0]

        def xw_load(src_ap, reads=(), eng="sync"):
            sl = xw_cur[0] % 4
            xw_cur[0] += 1
            em.dma(eng, XW[:, sl * 1024:(sl + 1) * 1024], src_ap,
                   "XW%d" % sl, reads=reads, writes=[rXW[sl]])
            return XW[:, sl * 1024:(sl + 1) * 1024], rXW[sl]

        def mm(bank_ap, lhsT, rhs, bank_res, reads, start, stop):
            def go(e, b=bank_ap, l=lhsT, r=rhs, st=start, sp=stop):
                return e.matmul(b, l, r, start=st, stop=sp, skip_group_check=True)
            return em.emit("tensor", go, reads=reads, writes=[bank_res], inc=("T", 1))

        def mmgroup(bank_ap, pieces, bank_res, reads):
            n = len(pieces)

            def go(e, b=bank_ap, pcs=pieces):
                last = None
                for i, (l, r) in enumerate(pcs):
                    last = e.matmul(b, l, r, start=(i == 0), stop=(i == n - 1),
                                    skip_group_check=True)
                return last
            return em.emit("tensor", go, reads=reads, writes=[bank_res], inc=("T", 1))

        def mmgroup2(bankA, piecesA, bankB, piecesB, resA, resB, reads):
            nA, nB = len(piecesA), len(piecesB)

            def go(e, a=bankA, b=bankB, pA=piecesA, pB=piecesB):
                for i, (l, r) in enumerate(pA):
                    e.matmul(a, l, r, start=(i == 0), stop=(i == nA - 1),
                             skip_group_check=True)
                last = None
                for i, (l, r) in enumerate(pB):
                    last = e.matmul(b, l, r, start=(i == 0), stop=(i == nB - 1),
                                    skip_group_check=True)
                return last
            return em.emit("tensor", go, reads=reads, writes=[resA, resB],
                           inc=("T", 1))

        def mmmulti(chunks, resl, reads):
            def go(e, cs=chunks):
                last = None
                for b, l, r, st, sp in cs:
                    last = e.matmul(b, l, r, start=st, stop=sp,
                                    skip_group_check=True)
                return last
            return em.emit("tensor", go, reads=reads, writes=resl, inc=("T", 1))

        em.emit("vector", lambda e: e.memset(ONE128[:], 1.0), writes=[rCONST], inc=("V", 1))
        em.emit("vector", lambda e: e.memset(ONE128B[:], 1.0 / 2048.0), writes=[rCONST], inc=("V", 1))
        for slot in range(2):
            for hh in range(2):
                em.emit("vector", lambda e, c=slot * 260 + hh * 130 + 128:
                        e.memset(VTL[:, c:c + 1], 1.0), writes=[rVTL[slot]], inc=("V", 1))
        em.emit("vector", lambda e: e.memset(CEPS5, EPS_LN), writes=[rCONST], inc=("V", 1))
        em.emit("vector", lambda e: e.memset(CEPS6, EPS_Z), writes=[rCONST], inc=("V", 1))
        em.emit("vector", lambda e: e.memset(CNEG1, -1.0), writes=[rCONST], inc=("V", 1))
        em.emit("vector", lambda e: e.memset(CNEGH, -0.5), writes=[rCONST], inc=("V", 1))



        def ln_finalize(banks, bres):
            for th in range(2):
                ts = slice(th * 512, (th + 1) * 512)
                em.emit("vector", lambda e, th=th, ts=ts, banks=banks: e.tensor_copy(
                    out=MSB[:, ts], in_=banks[th][:, 0:512]),
                    reads=[bres[th]], writes=[rMSB], inc=("V", 1))
            for th in range(2):
                ts = slice(th * 512, (th + 1) * 512)
                em.emit("vector", lambda e, th=th, ts=ts: e.tensor_tensor(
                    out=VT0[:, th * 512:(th + 1) * 512], in0=MSB[:, ts],
                    in1=MSB[:, ts], op=ALU.mult),
                    reads=[rMSB], writes=[rVT0a], inc=("V", 1))
            for th in range(2):
                em.emit("vector", lambda e, th=th, banks=banks: e.tensor_tensor(
                    out=SILU[:, th * 512:(th + 1) * 512], in0=banks[2 + th][:, 0:512],
                    in1=VT0[:, th * 512:(th + 1) * 512], op=ALU.subtract),
                    reads=[bres[2 + th], rVT0a], writes=[rVT0b, rSILU], inc=("V", 1))
            for th in range(2):
                em.emit("scalar", lambda e, th=th: e.activation(
                    out=SILU[:, th * 512:(th + 1) * 512],
                    in_=SILU[:, th * 512:(th + 1) * 512], func=AF.Ln,
                    bias=CEPS5), reads=[rCONST], writes=[rVT0b, rSILU], inc=("S", 1))
            for th in range(2):
                ts = slice(th * 512, (th + 1) * 512)
                em.emit("scalar", lambda e, ts=ts, th=th: e.activation(
                    out=RSB[:, ts], in_=SILU[:, th * 512:(th + 1) * 512],
                    func=AF.Exp,
                    scale=CNEGH), reads=[rVT0b, rSILU, rCONST], writes=[rRSB], inc=("S", 1))
            em.emit("vector", lambda e: e.tensor_copy(
                out=MSB[:, 1024:2048], in_=MSB[:, 0:1024]),
                reads=[rMSB], writes=[rMSB], inc=("V", 1))
            em.emit("vector", lambda e: e.tensor_copy(
                out=RSB[:, 1024:2048], in_=RSB[:, 0:1024]),
                reads=[rRSB], writes=[rRSB], inc=("V", 1))

        for c in range(NCD):
            sl = c % 2
            em.dma("gpsimd", XB16[:, c * 1024:(c + 1) * 1024],
                   xT[c * 128:(c + 1) * 128, :], "XB%d" % (c % 4),
                   writes=[rXB16c[c], rXBL[c % 4]])
            em.emit("vector", lambda e, c=c, sl=sl: e.tensor_tensor(
                out=XSQ[:, sl * 1024:(sl + 1) * 1024],
                in0=XB16[:, c * 1024:(c + 1) * 1024],
                in1=XB16[:, c * 1024:(c + 1) * 1024], op=ALU.mult),
                reads=[rXB16c[c]], writes=[rXSQ[sl]], inc=("V", 1))
            for th in range(2):
                mm(PB[th][:, 0:512], ONE128B[:],
                   XB16[:, c * 1024 + th * 512: c * 1024 + (th + 1) * 512],
                   rPB[th], reads=[rXB16c[c], rCONST],
                   start=(c == 0), stop=(c == NCD - 1))
            for th in range(2):
                mm(PB[2 + th][:, 0:512], ONE128B[:],
                   XSQ[:, sl * 1024 + th * 512: sl * 1024 + (th + 1) * 512],
                   rPB[2 + th], reads=[rXSQ[sl], rCONST],
                   start=(c == 0), stop=(c == NCD - 1))
        em.dma("gpsimd", WB[:, 0:8192], wkvh[0:128, :], "WB0",
               writes=[rW[0], rW[1]])
        em.dma("gpsimd", WB[:, 8192:16384], wkvh[128:256, :], "WB1",
               writes=[rW[2], rW[3]])
        em.dma("sync", decC[:], decC_h, "CST", writes=[rDECC])
        em.dma("sync", DECB[:], decB_h, "CST", writes=[rDECB])
        em.dma("sync", bqC[:], bqC_h, "CST", writes=[rCTAB])
        em.dma("sync", boC[:], boC_h, "CST", writes=[rCTAB])
        em.dma("sync", gsC[:], gsC_h, "CST", writes=[rCTAB])
        em.dma("sync", usC[:], usC_h, "CST", writes=[rCTAB])
        em.dma("sync", dsC[:], dsC_h, "CST", writes=[rCTAB])
        ln_finalize(PB, rPB)

        for cp in range(NCD // 2):
            em.emit("vector", lambda e, cp=cp: e.tensor_tensor(
                out=VT0B[:], in0=XB16[:, cp * 2048:(cp + 1) * 2048],
                in1=MSB[:, 0:2048],
                op=ALU.subtract), reads=[rXB16c[2 * cp], rXB16c[2 * cp + 1], rMSB],
                writes=[rVT0a, rVT0b], inc=("V", 1))
            em.emit("vector", lambda e, cp=cp: e.tensor_tensor(
                out=LNX[:, cp * 2048:(cp + 1) * 2048], in0=VT0B[:],
                in1=RSB[:, 0:2048], op=ALU.mult),
                reads=[rVT0a, rVT0b, rRSB], writes=[rLNXq[cp // 2]], inc=("V", 1))

        def cc_trigger(i):
            em.emit("gpsimd", lambda e, i=i: e.collective_compute(
                "AllReduce", ALU.add, replica_groups=RG,
                ins=[kvin_t[i].ap().opt()], outs=[kvout_t[i].ap().opt()]),
                reads=[rKVIN[i]], inc=("CC", 1))

        def bcs_load(s):
            sl2 = s % 2
            em.dma("sync", BCS[:, sl2 * 512:(sl2 + 1) * 512],
                   bkvB_h[:, s * 512:(s + 1) * 512], "BK%d" % sl2,
                   writes=[rBCS[sl2]])

        bcs_load(0)
        for s in range(NES):
            sl2 = s % 2
            if s + 1 < NES:
                bcs_load(s + 1)
            bigs = s % 2
            wslot = WB[:, bigs * 8192:(bigs + 1) * 8192]
            wrs = [rW[2 * bigs], rW[2 * bigs + 1]]
            if s == 6:
                cc_trigger(0)
            if s >= 2:
                em.dma("gpsimd", wslot, wkvh[s * 128:(s + 1) * 128, :],
                       "WB%d" % bigs, writes=wrs)

            def wk(c, w=wslot):
                return w[:, c * 512: c * 512 + 256]

            def wv(c, w=wslot):
                return w[:, c * 512 + 256: (c + 1) * 512]

            def kpieces(tch, c0, c1, wslot=wslot):
                kbank = PB[tch % 2]
                return [(kbank[:, 0:256],
                         LNX[:, c * 1024 + tch * 128: c * 1024 + tch * 128 + 128],
                         wk(c, wslot), c == 0, c == NCD - 1)
                        for c in range(c0, c1)]

            def vpieces(tch, c0, c1, wslot=wslot):
                vbank = PB[2 + tch % 2]
                return [(vbank[:, 0:256],
                         LNX[:, c * 1024 + tch * 128: c * 1024 + tch * 128 + 128],
                         wv(c, wslot), c == 0, c == NCD - 1)
                        for c in range(c0, c1)]

            def k_evac(tch, sl2=sl2):
                kbank = PB[tch % 2]
                em.emit("vector", lambda e, kbank=kbank, sl2=sl2: e.tensor_tensor(
                    out=VT0[:, 0:256], in0=kbank[:, 0:256],
                    in1=BCS[:, sl2 * 512: sl2 * 512 + 256],
                    op=ALU.add), reads=[rPB[tch % 2], rBCS[sl2]],
                    writes=[rVT0a], inc=("V", 1))
                em.emit("scalar", lambda e, tch=tch: e.activation(
                    out=KT[:, tch * 256:(tch + 1) * 256], in_=VT0[:, 0:256],
                    func=AF.Relu, scale=decC[:, tch:tch + 1]),
                    reads=[rVT0a, rDECC], writes=[rKT], inc=("S", 1))

            def v_evac(tch, sl2=sl2):
                vslot = tch % 2
                vbank = PB[2 + tch % 2]
                em.emit("vector", lambda e, vbank=vbank, vbase=vslot * 260, sl2=sl2: e.tensor_tensor(
                    out=VTL[:, vbase:vbase + 260].rearrange(
                        "p (h e) -> p h e", h=2)[:, :, 0:128],
                    in0=vbank[:, 0:256].rearrange("p (h e) -> p h e", h=2),
                    in1=BCS[:, sl2 * 512 + 256: sl2 * 512 + 512].rearrange(
                        "p (h e) -> p h e", h=2),
                    op=ALU.add), reads=[rPB[2 + tch % 2], rBCS[sl2]],
                    writes=[rVTL[vslot]], inc=("V", 1))

            def kv_mms(tch, s=s):
                vbase = (tch % 2) * 260
                for hh in range(2):
                    hg = s * 2 + hh
                    mm(PK[hg % 4][:, 0:129],
                       KT[:, tch * 256 + hh * 128: tch * 256 + hh * 128 + 128],
                       VTL[:, vbase + hh * 130: vbase + hh * 130 + 129],
                       rPK[hg % 4], reads=[rKT, rVTL[tch % 2]],
                       start=(tch == 0), stop=(tch == 7))

            if s == 0:
                for q in range(4):
                    for tch in (0, 1):
                        mmmulti(kpieces(tch, 4 * q, 4 * q + 4),
                                [rPB[tch % 2]], reads=[rLNXq[q]] + wrs)
                    for tch in (0, 1):
                        mmmulti(vpieces(tch, 4 * q, 4 * q + 4),
                                [rPB[2 + tch % 2]], reads=[rLNXq[q]] + wrs)
                for tch in (0, 1):
                    k_evac(tch)
                for tch in (0, 1):
                    v_evac(tch)
                for tch in range(2, 8):
                    mmmulti(kpieces(tch, 0, NCD), [rPB[tch % 2]],
                            reads=rLNXq + wrs)
                    k_evac(tch)
                kv_mms(0)
                for tch in range(2, 8):
                    mmmulti(vpieces(tch, 0, NCD), [rPB[2 + tch % 2]],
                            reads=rLNXq + wrs)
                    v_evac(tch)
                    kv_mms(tch - 1)
                kv_mms(7)
            else:
                for tch in range(8):
                    mmmulti(kpieces(tch, 0, NCD), [rPB[tch % 2]],
                            reads=rLNXq + wrs)
                    k_evac(tch)
                for tch in range(8):
                    mmmulti(vpieces(tch, 0, NCD), [rPB[2 + tch % 2]],
                            reads=rLNXq + wrs)
                    v_evac(tch)
                    if tch > 0:
                        kv_mms(tch - 1)
                kv_mms(7)
            for hh in range(2):
                hg = s * 2 + hh
                cs = slice(hg * 129, hg * 129 + 129)
                em.emit("vector", lambda e, hg=hg, cs=cs: e.tensor_copy(
                    out=KVS[:, cs], in_=PK[hg % 4][:, 0:129]),
                    reads=[rPK[hg % 4]], writes=[rKVS[s // 4]], inc=("V", 1))
            if s == 3:
                em.dma("sync", kvin[0], KVS[:, 0:1032], "KVI",
                       reads=[rKVS[0]], writes=[rKVIN[0]])

        def q_dma(g):
            sl = g % 4
            em.dma("gpsimd", WB[:, sl * 4096:(sl + 1) * 4096],
                   wqh[g * 128:(g + 1) * 128, :], "W%d" % sl, writes=[rW[sl]])

        q_dma(0)
        q_dma(1)
        q_dma(2)

        em.dma("sync", kvin[1], KVS[:, 1032:2064], "KVI",
               reads=[rKVS[1]], writes=[rKVIN[1]])
        cc_trigger(1)
        em.dma("scalar", KVS[:, 0:1032], kvout[0], "KVO", waits=[("CC", 1)],
               writes=[rKVS[0]])
        em.dma("sync", KVS[:, 1032:2064], kvout[1], "KVO", waits=[("CC", 2)],
               writes=[rKVS[1]])

        def q_stage(h):
            if h % 2 == 0 and h >= 6:
                q_dma(h // 2)
            sl = (h // 2) % 4
            qs = h % NQS
            qb = 2 * (h % 2)
            pcs = [[(WB[:, sl * 4096 + c * 256 + (h % 2) * 128:
                        sl * 4096 + c * 256 + (h % 2) * 128 + 128],
                     LNX[:, c * 1024 + th * 512: c * 1024 + th * 512 + 512])
                    for c in range(NCD)] for th in range(2)]
            mmgroup2(PB[qb][:, 0:512], pcs[0], PB[qb + 1][:, 0:512], pcs[1],
                     rPB[qb], rPB[qb + 1], reads=[rW[sl]] + rLNXq)
            for th in range(2):
                vt = rVT0a if th == 0 else rVT0b
                em.emit("scalar", lambda e, th=th, h=h, qb=qb: e.activation(
                    out=VT0[:, th * 512:(th + 1) * 512], in_=PB[qb + th][:, 0:512],
                    func=AF.Relu, bias=bqC[:, h:h + 1]),
                    reads=[rPB[qb + th], rCTAB], writes=[vt], inc=("S", 1))
                em.emit("vector", lambda e, th=th, qs=qs: e.tensor_tensor(
                    out=QTB[:, qs * 1024 + th * 512: qs * 1024 + th * 512 + 512],
                    in0=VT0[:, th * 512:(th + 1) * 512],
                    in1=DECB[:, th * 512:(th + 1) * 512], op=ALU.mult),
                    reads=[vt, rDECB], writes=[rQTB[qs]], inc=("V", 1))

        def attn_stage(h):
            qs = h % NQS
            hf = h // 8
            for th in range(2):
                qsl = slice(qs * 1024 + th * 512, qs * 1024 + th * 512 + 512)
                mm(PK[2 + th][:, 0:512], KSRF[:, h * 128:(h + 1) * 128],
                   QTB[:, qsl], rPK[2 + th],
                   reads=[rKSRF[hf], rQTB[qs]], start=True, stop=True)
            for th in range(2):
                zs = th + 2 * (h % 2)
                em.emit("scalar", lambda e, th=th, zs=zs: e.activation(
                    out=ZBF[:, zs * 512:(zs + 1) * 512], in_=PK[2 + th][:, 0:512],
                    func=AF.Ln, bias=CEPS6),
                    reads=[rPK[2 + th], rCONST], writes=[rZBF[zs]], inc=("S", 1))
            for th in range(2):
                zs = th + 2 * (h % 2)
                em.emit("scalar", lambda e, zs=zs: e.activation(
                    out=ZBF[:, zs * 512:(zs + 1) * 512],
                    in_=ZBF[:, zs * 512:(zs + 1) * 512],
                    func=AF.Exp, scale=CNEG1),
                    reads=[rCONST], writes=[rZBF[zs]], inc=("S", 1))
            for th in range(2):
                qsl = slice(qs * 1024 + th * 512, qs * 1024 + th * 512 + 512)
                mm(PK[th][:, 0:512], KVB[:, h * 129: h * 129 + 128], QTB[:, qsl],
                   rPK[th], reads=[rKVB[hf], rQTB[qs]], start=True, stop=True)
            for th in range(2):
                zs = th + 2 * (h % 2)
                em.emit("vector", lambda e, th=th, h=h, zs=zs: e.tensor_tensor(
                    out=REGB[:, h * 1024 + th * 512: h * 1024 + th * 512 + 512],
                    in0=PK[th][:, 0:512], in1=ZBF[:, zs * 512:(zs + 1) * 512],
                    op=ALU.mult), reads=[rPK[th], rZBF[zs]],
                    writes=[rREGBq[h // 4]], inc=("V", 1))

        def kv_finish(hf):
            em.emit("scalar", lambda e, hf=hf: e.activation(
                out=KVB[:, hf * 1032:(hf + 1) * 1032],
                in_=KVS[:, hf * 1032:(hf + 1) * 1032], func=AF.Identity),
                reads=[rKVS[hf]], writes=[rKVB[hf]], inc=("S", 1))
            for h in range(hf * 8, hf * 8 + 8):
                em.emit("vector", lambda e, h=h: e.tensor_scalar(
                    out=KSRF[:, h * 128:(h + 1) * 128], in0=ONE128[:],
                    scalar1=KVS[:, h * 129 + 128: h * 129 + 129],
                    scalar2=None, op0=ALU.mult),
                    reads=[rKVS[hf], rCONST], writes=[rKSRF[hf]], inc=("V", 1))

        kv_finish(0)
        for h in range(QLOOK + 1):
            q_stage(h)
        for h in range(H):
            attn_stage(h)
            if h == 3:
                kv_finish(1)
            if h + QLOOK + 1 < H:
                q_stage(h + QLOOK + 1)

        def x1_stats(j):
            x1sl = j % 2
            for th in range(2):
                mm(PK[th][:, 0:512], ONE128B[:],
                   LNX[:, j * 1024 + th * 512: j * 1024 + (th + 1) * 512],
                   rPK[th], reads=[rLNXc[j], rCONST],
                   start=(j == 0), stop=(j == NCD - 1))
            for th in range(2):
                mm(PK[2 + th][:, 0:512], ONE128B[:],
                   XSQ[:, x1sl * 1024 + th * 512: x1sl * 1024 + (th + 1) * 512],
                   rPK[2 + th], reads=[rXSQ[x1sl], rCONST],
                   start=(j == 0), stop=(j == NCD - 1))

        for j in range(NCD):
            if j % 2 == 0:
                g2 = j // 2
                sl = g2 % 2
                em.dma("gpsimd", WB[:, sl * 4096:(sl + 1) * 4096],
                       woh[g2 * 128:(g2 + 1) * 128, :], "W%d" % sl,
                       writes=[rW[sl]])
            sl = (j // 2) % 2
            ob = 2 * (j % 2)
            xwap, rxw = xw_load(xT[j * 128:(j + 1) * 128, :], eng="scalar")
            pcs = [[(WB[:, sl * 4096 + c * 256 + (j % 2) * 128:
                        sl * 4096 + c * 256 + (j % 2) * 128 + 128],
                     REGB[:, c * 1024 + th * 512: c * 1024 + th * 512 + 512])
                    for c in range(NCD)] for th in range(2)]
            mmgroup2(PB[ob][:, 0:512], pcs[0], PB[ob + 1][:, 0:512], pcs[1],
                     rPB[ob], rPB[ob + 1], reads=[rW[sl]] + rREGBq)
            if j > 0:
                x1_stats(j - 1)
            x1sl = j % 2
            for th in range(2):
                vt = rVT0a if th == 0 else rVT0b
                em.emit("scalar", lambda e, th=th, j=j, ob=ob: e.activation(
                    out=VT0[:, th * 512:(th + 1) * 512], in_=PB[ob + th][:, 0:512],
                    func=AF.Identity, bias=boC[:, j:j + 1]),
                    reads=[rPB[ob + th], rCTAB], writes=[vt], inc=("S", 1))
                em.emit("vector", lambda e, th=th, x1sl=x1sl, xwap=xwap: e.tensor_tensor(
                    out=X1W[:, x1sl * 1024 + th * 512: x1sl * 1024 + (th + 1) * 512],
                    in0=VT0[:, th * 512:(th + 1) * 512],
                    in1=xwap[:, th * 512:(th + 1) * 512], op=ALU.add),
                    reads=[vt, rxw], writes=[rX1W[x1sl]], inc=("V", 1))
            em.dma("sync", x1d[j * 128:(j + 1) * 128, :],
                   X1W[:, x1sl * 1024:(x1sl + 1) * 1024], "X1S%d" % x1sl,
                   reads=[rX1W[x1sl]], writes=[rX1L[x1sl], rX1D[j]])
            em.emit("vector", lambda e, j=j, x1sl=x1sl: e.tensor_copy(
                out=LNX[:, j * 1024:(j + 1) * 1024],
                in_=X1W[:, x1sl * 1024:(x1sl + 1) * 1024]),
                reads=[rX1W[x1sl]], writes=[rLNXc[j]], inc=("V", 1))
            em.emit("vector", lambda e, j=j, x1sl=x1sl: e.tensor_tensor(
                out=XSQ[:, x1sl * 1024:(x1sl + 1) * 1024],
                in0=LNX[:, j * 1024:(j + 1) * 1024],
                in1=LNX[:, j * 1024:(j + 1) * 1024], op=ALU.mult),
                reads=[rLNXc[j]], writes=[rXSQ[x1sl]], inc=("V", 1))
        x1_stats(NCD - 1)
        ln_finalize(PK, rPK)

        for cp in range(NCD // 2):
            em.emit("vector", lambda e, cp=cp: e.tensor_tensor(
                out=VT0B[:], in0=LNX[:, cp * 2048:(cp + 1) * 2048],
                in1=MSB[:, 0:2048],
                op=ALU.subtract), reads=[rLNXc[2 * cp], rLNXc[2 * cp + 1], rMSB],
                writes=[rVT0a, rVT0b], inc=("V", 1))
            em.emit("vector", lambda e, cp=cp: e.tensor_tensor(
                out=REGB[:, cp * 2048:(cp + 1) * 2048], in0=VT0B[:],
                in1=RSB[:, 0:2048], op=ALU.mult),
                reads=[rVT0a, rVT0b, rRSB], writes=[rREGBq[cp // 2]], inc=("V", 1))

        def wgu_dma(ic):
            sl = 2 + ic % 2
            em.dma("gpsimd", WB[:, sl * 4096:(sl + 1) * 4096],
                   wguh[ic * 128:(ic + 1) * 128, :], "W%d" % sl,
                   writes=[rW[sl]])

        wgu_dma(0)
        wgu_dma(1)
        em.dma("gpsimd", WB[:, 0:5632], wdnh[0:128, :], "WB0",
               writes=[rW[0], rW[1]])
        for ic in range(NIC):
            sl = 2 + ic % 2
            if ic >= 2:
                wgu_dma(ic)
            a = 2 * (ic % 2)
            G = [PB[a], PB[a + 1]]
            U = [PK[a], PK[a + 1]]

            def gu_chunks(c0, c1):
                chunks = []
                for c in range(c0, c1):
                    gap = WB[:, sl * 4096 + c * 256: sl * 4096 + c * 256 + 128]
                    uap = WB[:, sl * 4096 + c * 256 + 128: sl * 4096 + (c + 1) * 256]
                    st, sp = (c == 0), (c == NCD - 1)
                    for th in range(2):
                        chunks.append((G[th][:, 0:512], gap,
                                       REGB[:, c * 1024 + th * 512: c * 1024 + (th + 1) * 512],
                                       st, sp))
                    for th in range(2):
                        chunks.append((U[th][:, 0:512], uap,
                                       REGB[:, c * 1024 + th * 512: c * 1024 + (th + 1) * 512],
                                       st, sp))
                return chunks
            bres = [rPB[a], rPB[a + 1], rPK[a], rPK[a + 1]]
            if ic <= 1:
                for q in range(4):
                    mmmulti(gu_chunks(4 * q, 4 * q + 4), bres,
                            reads=[rW[sl], rREGBq[q]])
            else:
                mmmulti(gu_chunks(0, NCD), bres, reads=[rW[sl]] + rREGBq)
            for th in range(2):
                em.emit("scalar", lambda e, th=th, ic=ic, G=G: e.activation(
                    out=SILU[:, th * 512:(th + 1) * 512], in_=G[th][:, 0:512],
                    func=AF.Silu, scale=gsC[:, ic:ic + 1]),
                    reads=[rPB[a + th], rCTAB], writes=[rSILU], inc=("S", 1))
            for th in range(2):
                em.emit("vector", lambda e, th=th, ic=ic, U=U: e.scalar_tensor_tensor(
                    out=HDN[:, ic * 1024 + th * 512: ic * 1024 + (th + 1) * 512],
                    in0=U[th][:, 0:512], scalar=usC[:, ic:ic + 1],
                    in1=SILU[:, th * 512:(th + 1) * 512],
                    op0=ALU.mult, op1=ALU.mult),
                    reads=[rPK[a + th], rSILU, rCTAB], writes=[rHDN[ic]],
                    inc=("V", 1))

        em.dma("gpsimd", WB[:, 8192: 8192 + 5632], wdnh[128:256, :], "WB1",
               writes=[rW[2], rW[3]])
        BPAIRS = [(PB[0], PB[1], rPB[0], rPB[1]), (PB[2], PB[3], rPB[2], rPB[3]),
                  (PK[0], PK[1], rPK[0], rPK[1]), (PK[2], PK[3], rPK[2], rPK[3])]
        for j in range(NCD):
            bigs = j % 2
            if j >= 2:
                em.dma("gpsimd", WB[:, bigs * 8192: bigs * 8192 + 5632],
                       wdnh[j * 128:(j + 1) * 128, :], "WB%d" % bigs,
                       writes=[rW[2 * bigs], rW[2 * bigs + 1]])
            wrs = [rW[2 * bigs], rW[2 * bigs + 1]]
            xwap, rxw = xw_load(x1d[j * 128:(j + 1) * 128, :], reads=[rX1D[j]])
            b0, b1, rb0, rb1 = BPAIRS[j % 4]
            chunks = []
            for ic in range(NIC):
                dap = WB[:, bigs * 8192 + ic * 128: bigs * 8192 + (ic + 1) * 128]
                st, sp = (ic == 0), (ic == NIC - 1)
                chunks.append((b0[:, 0:512], dap, HDN[:, ic * 1024: ic * 1024 + 512],
                               st, sp))
                chunks.append((b1[:, 0:512], dap, HDN[:, ic * 1024 + 512: (ic + 1) * 1024],
                               st, sp))
            mmmulti(chunks, [rb0, rb1], reads=wrs + rHDN)
            x1sl = j % 2
            for th in range(2):
                bb = b0 if th == 0 else b1
                rb = rb0 if th == 0 else rb1
                em.emit("vector", lambda e, th=th, j=j, x1sl=x1sl, bb=bb, xwap=xwap: e.scalar_tensor_tensor(
                    out=X1W[:, x1sl * 1024 + th * 512: x1sl * 1024 + (th + 1) * 512],
                    in0=bb[:, 0:512], scalar=dsC[:, j:j + 1],
                    in1=xwap[:, th * 512:(th + 1) * 512],
                    op0=ALU.mult, op1=ALU.add),
                    reads=[rb, rxw, rCTAB], writes=[rX1W[x1sl]], inc=("V", 1))
            em.dma("sync", y[j * 128:(j + 1) * 128, :],
                   X1W[:, x1sl * 1024:(x1sl + 1) * 1024], "Y%d" % (j % 2),
                   reads=[rX1W[x1sl]], writes=[rYL[j % 2]])

        fin = [(lane, em.cnt[lane]) for lane in ("Y0", "Y1") if em.cnt[lane] > 0]

        def emit_engine(eng_name):
            def go(e):
                for fn, ws, inc in em.ops[eng_name]:
                    for sem, val in ws:
                        e.wait_ge(semmap[sem], val)
                    inst = fn(e)
                    if inc is not None:
                        inst.then_inc(semmap[inc[0]], inc[1])
                if eng_name == "sync":
                    for lane, val in fin:
                        e.wait_ge(semmap[lane], val)
            return go

        block.sync(emit_engine("sync"))
        block.gpsimd(emit_engine("gpsimd"))
        block.tensor(emit_engine("tensor"))
        block.vector(emit_engine("vector"))
        block.scalar(emit_engine("scalar"))

    return nc


_NC_CACHE = {}


def kernel(**inputs):
    x = np.asarray(inputs["L0_attn_x"], dtype=np.float32)
    Wq = np.asarray(inputs["L0_attn_Wq"], dtype=np.float32)
    bq = np.asarray(inputs["L0_attn_bq"], dtype=np.float32)
    Wk = np.asarray(inputs["L0_attn_Wk"], dtype=np.float32)
    bk = np.asarray(inputs["L0_attn_bk"], dtype=np.float32)
    Wv = np.asarray(inputs["L0_attn_Wv"], dtype=np.float32)
    bv = np.asarray(inputs["L0_attn_bv"], dtype=np.float32)
    Wo = np.asarray(inputs["L0_attn_Wo"], dtype=np.float32)
    bo = np.asarray(inputs["L0_attn_bo"], dtype=np.float32)
    gw = np.asarray(inputs["L0_ffn_gate_w"], dtype=np.float32)
    uw = np.asarray(inputs["L0_ffn_up_w"], dtype=np.float32)
    dw = np.asarray(inputs["L0_ffn_down_w"], dtype=np.float32)
    gs = np.asarray(inputs["L0_ffn_gate_s"], dtype=np.float32)
    us = np.asarray(inputs["L0_ffn_up_s"], dtype=np.float32)
    ds = np.asarray(inputs["L0_ffn_down_s"], dtype=np.float32)

    wkT = Wk.T.reshape(NCD, 128, NES, 256).transpose(2, 1, 0, 3)
    wvT = Wv.T.reshape(NCD, 128, NES, 256).transpose(2, 1, 0, 3)
    wkv = np.empty((NES, 128, NCD, 512), dtype=np.float32)
    wkv[:, :, :, 0:256] = wkT
    wkv[:, :, :, 256:512] = wvT
    wqt = np.ascontiguousarray(
        Wq.T.reshape(NCD, 128, 8, 256).transpose(2, 1, 0, 3)).reshape(8 * 128, NCD * 256)
    wot = np.ascontiguousarray(
        Wo.T.reshape(NCD, 128, 8, 256).transpose(2, 1, 0, 3)).reshape(8 * 128, NCD * 256)
    ggT = gw.T.reshape(NCD, 128, NIC, 128).transpose(2, 1, 0, 3)
    uuT = uw.T.reshape(NCD, 128, NIC, 128).transpose(2, 1, 0, 3)
    wgu = np.empty((NIC, 128, NCD, 256), dtype=np.float32)
    wgu[:, :, :, 0:128] = ggT
    wgu[:, :, :, 128:256] = uuT
    wdn = np.ascontiguousarray(
        dw.T.reshape(NIC, 128, NCD, 128).transpose(2, 1, 0, 3)).reshape(16 * 128, NIC * 128)
    bkv = np.empty((128, NES, 512), dtype=np.float32)
    bkv[:, :, 0:256] = np.broadcast_to(bk.reshape(NES, 256), (128, NES, 256))
    bkv[:, :, 256:512] = np.broadcast_to(bv.reshape(NES, 256), (128, NES, 256))

    shared = dict(
        wkvh=np.ascontiguousarray(wkv).reshape(NES * 128, NCD * 512),
        wqh=wqt, woh=wot,
        wguh=np.ascontiguousarray(wgu).reshape(NIC * 128, NCD * 256),
        wdnh=wdn,
        bkvB=np.ascontiguousarray(bkv).reshape(128, NES * 512),
        bqC=np.ascontiguousarray(bq.reshape(16, 128).T),
        boC=np.ascontiguousarray(bo.reshape(16, 128).T),
        gsC=np.ascontiguousarray(gs.reshape(NIC, 128).T),
        usC=np.ascontiguousarray(us.reshape(NIC, 128).T),
        dsC=np.ascontiguousarray(ds.reshape(16, 128).T))

    in_maps = []
    for ci in range(8):
        b, h2 = ci // 2, ci % 2
        xTc = np.ascontiguousarray(x[b, h2 * SO:(h2 + 1) * SO, :].T)
        pos = h2 * SO + np.arange(SO)
        dec = np.exp(-pos / float(SF)).astype(np.float32)
        in_maps.append(dict(
            shared, xT=xTc,
            decC=np.ascontiguousarray(dec.reshape(8, 128).T),
            decB=np.ascontiguousarray(np.broadcast_to(dec, (128, SO)))))

    if "nc" not in _NC_CACHE:
        _NC_CACHE["nc"] = build_program()
    nc = _NC_CACHE["nc"]
    res = run_bass_kernel_spmd(nc, in_maps, core_ids=list(range(8)))

    out = np.empty((4, SF, D), dtype=np.float32)
    for ci in range(8):
        b, h2 = ci // 2, ci % 2
        out[b, h2 * SO:(h2 + 1) * SO, :] = res.results[ci]["y"].T
    return out
